# version 02
# baseline (speedup 1.0000x reference)
import numpy as np
import ml_dtypes
import concourse.bass as bass
import concourse.mybir as mybir

F32 = mybir.dt.float32
BF16 = mybir.dt.bfloat16
U8 = mybir.dt.uint8
AF = mybir.ActivationFunctionType
ALU = mybir.AluOpType
AX = mybir.AxisListType
f32 = np.float32

D = 2048
L = 2048
TL = 2560
NT = 20
NKC = 16
FE = 1024
E = 8

ENGINES = ("sync", "gpsimd", "scalar", "vector", "tensor")


class B:
    def __init__(self, nc):
        self.nc = nc
        self.recs = {e: [] for e in ENGINES}
        self.sems = {}
        self.counts = {}
        self.dma_sems = set()
        self.maxwait = {e: {} for e in ENGINES}
        self.last_ev = {}

    def sem(self, name, dma=False):
        if name not in self.sems:
            self.sems[name] = self.nc.alloc_semaphore(name)
            self.counts[name] = 0
            if dma:
                self.dma_sems.add(name)
        return name

    def op(self, eng, fn, waits=(), inc=None, n=1):
        w = []
        for ev in waits:
            if ev is None:
                continue
            (s, v, peng) = ev
            if v <= 0:
                continue
            if self.maxwait[eng].get(s, -1) >= v:
                continue
            self.maxwait[eng][s] = v
            w.append((s, v))
        evout = None
        if inc is not None:
            self.sem(inc)
            self.counts[inc] += n
            evout = (inc, self.counts[inc], eng)
            self.last_ev[eng] = evout
        self.recs[eng].append((fn, w, inc, n))
        return evout

    def dma(self, eng, out_ap, in_ap, waits=(), slow=False, accum=False):
        qsem = "q_" + eng
        self.sem(qsem, dma=True)
        def fn(e, out_ap=out_ap, in_ap=in_ap, slow=slow, accum=accum):
            kw = {}
            if slow:
                kw["allow_slow_non_contiguous"] = True
            if accum:
                kw["accum_op"] = ALU.add
            return e.dma_start(out=out_ap, in_=in_ap, **kw)
        return self.op(eng, fn, waits=waits, inc=qsem, n=16)

    def fence(self):
        return [self.last_ev.get(e) for e in ENGINES]

    def wait_only(self, eng, waits):
        self.recs[eng].append((None, [(s, v) for (s, v, _) in waits if v > 0], None, 0))

    def emit(self):
        nc = self.nc
        with nc.Block() as block:
            for eng_name in ENGINES:
                recs = self.recs[eng_name]
                def make(recs=recs):
                    def run(eng):
                        for (fn, w, inc, n) in recs:
                            for (s, v) in w:
                                eng.wait_ge(self.sems[s], v)
                            if fn is None:
                                continue
                            inst = fn(eng)
                            if inc is not None:
                                if isinstance(inst, list):
                                    inst = inst[-1]
                                inst.then_inc(self.sems[inc], n)
                    return run
                getattr(block, eng_name)(make())


def build(taps=()):
    nc = bass.Bass("TRN2", target_bir_lowering=False)
    b = B(nc)

    def din(name, shape, dt=F32):
        return nc.dram_tensor(name, shape, dt, kind="ExternalInput").ap()
    x_in = din("x_in", [TL, D])
    wq_in = din("wq", [D, D]); wk_in = din("wk", [512, D])
    wv_in = din("wv", [512, D]); wo_in = din("wo", [D, D])
    router_in = din("router", [8, D]); normw_in = din("normw", [1, D])
    gate_in = din("gate", [E, D, FE]); up_in = din("up", [E, D, FE])
    down_in = din("down", [E, FE, D])
    c_in = din("c_row", [D]); wmod_in = din("wmod", [D, 6 * D])
    cos_in = din("cos_in", [TL, 128]); sin_in = din("sin_in", [TL, 128])
    masks_in = din("masks_in", [8, 6, 128, 256], BF16)
    ident_in = din("ident_in", [128, 128], BF16)
    ones_in = din("ones_in", [128, 1], BF16)
    out = nc.dram_tensor("out", [L, D], F32, kind="ExternalOutput").ap()

    mod_hbm = nc.dram_tensor("mod_hbm", [6, D], F32, kind="Internal").ap()
    wqT_hbm = nc.dram_tensor("wqT_hbm", [NKC, 128, D], BF16, kind="Internal").ap()
    woT_hbm = nc.dram_tensor("woT_hbm", [NKC, 128, D], BF16, kind="Internal").ap()
    qt_hbm = nc.dram_tensor("qt_hbm", [8, NKC, 128, 256], BF16, kind="Internal").ap()
    x1_hbm = nc.dram_tensor("x1_hbm", [L, D], F32, kind="Internal").ap()
    y_hbm = nc.dram_tensor("y_hbm", [L, D], F32, kind="Internal").ap()

    tap_aps = {}
    def tap(name, shape, dtype=F32):
        if name in taps:
            tap_aps[name] = nc.dram_tensor("tap_" + name, shape, dtype, kind="ExternalOutput").ap()
            return tap_aps[name]
        return None

    ARENA = 212800
    arena = nc.alloc_sbuf_tensor("arena", [128, ARENA], U8)
    av = arena[:]
    cursor = [0]
    hwm = [0]
    def buf(nbytes, dt, shape_last):
        off = (cursor[0] + 63) // 64 * 64
        cursor[0] = off + nbytes
        hwm[0] = max(hwm[0], cursor[0])
        assert cursor[0] <= ARENA, f"arena overflow {cursor[0]}"
        ap = av[:, off:off + nbytes].bitcast(dt)
        tot = 1
        for s_ in shape_last:
            tot *= s_
        assert tot == ap.shape[1], (shape_last, ap.shape)
        if len(shape_last) == 1:
            return ap
        pat = "p (" + " ".join(f"d{i}" for i in range(len(shape_last))) + ") -> p " + " ".join(f"d{i}" for i in range(len(shape_last)))
        return ap.rearrange(pat, **{f"d{i}": s_ for i, s_ in enumerate(shape_last)})

    ident = buf(256, BF16, [128])
    ones_col = buf(64, BF16, [32])[:, 0:1]
    modbc = buf(16384, F32, [2, 2048])
    stats = buf(320, F32, [80])
    cwbuf = buf(128, F32, [4, 8])
    _ls = buf(64, F32, [16])
    logit_sb = _ls[:, 0:8]
    srt8 = _ls[:, 8:16]
    PERS_END = cursor[0]

    kT_sb = buf(20480, BF16, [4, 2560])
    v_sb = buf(20480, BF16, [20, 512])
    SPAN_END = cursor[0]
    PB = (SPAN_END + 63) // 64 * 64

    pool_t = nc.alloc_psum_tensor("pool", [128, 4096], F32)
    pool = pool_t[:]
    pool_bf = pool_t[:].bitcast(BF16)
    def bank(i):
        return pool[:, i * 512:(i + 1) * 512]
    def bank_bf(i):
        return pool_bf[:, i * 1024:(i + 1) * 1024]

    QS, QG = "q_sync", "q_gp"
    b.sem(QS, dma=True); b.sem(QG, dma=True)
    SV, SS, ST = "sv", "ss", "st"

    ev_ident = b.dma("sync", ident, ident_in)
    ev_ones = b.dma("sync", ones_col, ones_in)

    cursor[0] = PB
    wm_sb = buf(32768, BF16, [2, NKC, 512])
    c_sb = buf(64, F32, [16])
    c_si = buf(32, BF16, [16])
    mrow_region = buf(49152, F32, [12288])
    mrow = mrow_region[0:1, :]
    clampscr = buf(16384, F32, [4096])
    hi_row = clampscr[0:1, 0:2048]
    lo_row = clampscr[0:1, 2048:4096]
    P0_END = cursor[0]

    ev_c = b.dma("sync", c_sb, c_in.rearrange("(k p) -> p k", p=128), slow=True)
    ev_silu = b.op("scalar", lambda e: e.activation(out=c_si, in_=c_sb, func=AF.Silu),
                   waits=[ev_c], inc=SS)
    free_wm = [None, None]
    free_p0bank = [None, None]
    ev_evac = {}
    for j in range(24):
        sl = j % 2
        ev_l = b.dma("gpsimd", wm_sb[:, sl],
                     wmod_in[:, 512 * j:512 * j + 512].rearrange("(k p) c -> p k c", p=128),
                     waits=[free_wm[sl]])
        def mmj(e, sl=sl):
            insts = []
            for k in range(NKC):
                insts.append(e.matmul(bank(sl)[0:1, :], c_si[:, k:k + 1], wm_sb[:, sl, k],
                                      start=(k == 0), stop=(k == NKC - 1)))
            return insts
        ev_mm = b.op("tensor", mmj, waits=[ev_l, ev_silu, free_p0bank[sl]], inc=ST)
        free_wm[sl] = ev_mm
        ev_evac[j] = b.op("vector",
                          lambda e, sl=sl, j=j: e.tensor_copy(out=mrow[:, 512 * j:512 * j + 512],
                                                              in_=bank(sl)[0:1, :]),
                          waits=[ev_mm], inc=SV)
        free_p0bank[sl] = ev_evac[j]
    mr = mrow.rearrange("o (six d) -> o six d", six=6)
    ev_ca1 = b.op("vector", lambda e: e.tensor_scalar_add(out=mr[:, 1], in0=mr[:, 1], scalar1=1.0),
                  waits=[ev_evac[7]], inc=SV)
    ev_cb1 = b.op("vector", lambda e: e.tensor_scalar_add(out=mr[:, 4], in0=mr[:, 4], scalar1=1.0),
                  waits=[ev_evac[19]], inc=SV)
    def clamp(idx, wait):
        c1 = b.op("vector", lambda e, idx=idx: e.tensor_scalar_max(out=hi_row, in0=mr[:, idx], scalar1=0.25), waits=[wait], inc=SV)
        c2 = b.op("vector", lambda e, idx=idx: e.tensor_scalar_min(out=lo_row, in0=mr[:, idx], scalar1=-0.25), waits=[wait, c1], inc=SV)
        c3 = b.op("vector", lambda e: e.tensor_sub(out=hi_row, in0=hi_row, in1=lo_row), waits=[c1, c2], inc=SV)
        c4 = b.op("vector", lambda e, idx=idx: e.tensor_single_scalar(out=mr[:, idx], in_=mr[:, idx], scalar=0.0, op=ALU.is_ge), waits=[c3], inc=SV)
        c5 = b.op("vector", lambda e, idx=idx: e.tensor_mul(out=hi_row, in0=hi_row, in1=mr[:, idx]), waits=[c4], inc=SV)
        return b.op("vector", lambda e, idx=idx: e.tensor_add(out=mr[:, idx], in0=hi_row, in1=lo_row), waits=[c5], inc=SV)
    ev_ga = clamp(2, ev_evac[11])
    ev_gb = clamp(5, ev_evac[23])
    ev_modspill = b.dma("sync", mod_hbm.flatten()[None, :], mrow,
                        waits=[ev_ca1, ev_cb1, ev_ga, ev_gb])
    t_mod = tap("mod", [6, D])
    if t_mod is not None:
        b.dma("sync", t_mod.flatten()[None, :], mrow,
              waits=[ev_ca1, ev_cb1, ev_ga, ev_gb])
    def bc_mod(slot, idx, waits=()):
        return b.dma("sync", modbc[:, slot], mod_hbm[idx:idx + 1, :].broadcast_to([128, D]),
                     waits=[ev_modspill] + list(waits))
    ev_bc_ca1 = bc_mod(0, 1)
    ev_bc_sa = bc_mod(1, 0)
    t_mbc = tap("mbc", [128, 3, 2048])
    if t_mbc is not None:
        b.dma("sync", t_mbc.rearrange("p s d -> p (s d)").rearrange("p (s d) -> p s d", s=3),
              modbc[:], waits=[ev_bc_ca1, ev_bc_sa])

    cursor[0] = PB
    wkT_sb = buf(16384, BF16, [NKC, 512])
    wvT_sb = buf(16384, BF16, [NKC, 512])
    xg = buf(16384, F32, [2, 2048])
    scr_a = buf(8192, F32, [2048])
    h_bf = buf(4096, BF16, [1, 2048])
    hT_sb = buf(16384, BF16, [NKC, 512])
    cossin = buf(4096, F32, [4, 256])
    q_r = buf(2048, F32, [512])
    qln_scr = buf(1024, BF16, [512])
    sq_qk = buf(2048, F32, [512])
    kln_bf = qln_scr
    qt_stage = buf(16384, BF16, [NKC, 512])
    rope_scr = buf(2048, F32, [512])
    wpc = buf(32768, BF16, [2, NKC, 512])
    A1_END = cursor[0]
    wnat = buf(8192, BF16, [2, 2048])
    wt_stage = buf(4096, BF16, [NKC, 128])
    A0_END = cursor[0]
    assert A0_END <= ARENA, A0_END

    free_wnat = [None, None]
    free_stg = [None, None]
    free_wt = None
    seq = 0
    jobs = ([("ksb", wk_in, oc, wkT_sb) for oc in range(4)] +
            [("vsb", wv_in, oc, wvT_sb) for oc in range(4)] +
            [("qh", wq_in, oc, wqT_hbm) for oc in range(NKC)] +
            [("oh", wo_in, oc, woT_hbm) for oc in range(NKC)])
    ev_wkv_done = None
    ev_wq_spill = None
    ev_wo_spill = {}
    for (kind, src, oc, dst) in jobs:
        sl = seq % 2
        ev_l = b.dma("gpsimd", wnat[:, sl], src[128 * oc:128 * oc + 128, :],
                     waits=[free_wnat[sl]])
        last_evac = None
        for half in range(2):
            stg = half
            mms = []
            for fi in range(8):
                fc = 8 * half + fi
                psl = bank_bf(6 + stg)[:, fi * 128:fi * 128 + 128]
                mms.append(b.op("tensor",
                                lambda e, psl=psl, sl=sl, fc=fc: e.transpose(psl, wnat[:, sl, 128 * fc:128 * fc + 128], ident),
                                waits=[ev_l, ev_ident, free_stg[stg]], inc=ST))
            src_view = bank_bf(6 + stg)[:, 0:1024].rearrange("p (f c) -> p f c", f=8)
            if kind in ("ksb", "vsb"):
                last_evac = b.op("vector",
                                 lambda e, src_view=src_view, dst=dst, half=half, oc=oc: e.tensor_copy(
                                     out=dst[:, 8 * half:8 * half + 8, 128 * oc:128 * oc + 128], in_=src_view),
                                 waits=[mms[-1]], inc=SV)
            else:
                last_evac = b.op("vector",
                                 lambda e, src_view=src_view, half=half: e.tensor_copy(
                                     out=wt_stage[:, 8 * half:8 * half + 8, :], in_=src_view),
                                 waits=[mms[-1], free_wt], inc=SV)
            free_stg[stg] = last_evac
        free_wnat[sl] = mms[-1]
        if kind in ("qh", "oh"):
            evs = b.dma("sync", dst[:, :, 128 * oc:128 * oc + 128].rearrange("fc p c -> p fc c"),
                        wt_stage[:], waits=[last_evac])
            free_wt = evs
            if kind == "qh" and oc == NKC - 1:
                ev_wq_spill = evs
            if kind == "oh":
                ev_wo_spill[oc] = evs
        if kind == "vsb" and oc == 3:
            ev_wkv_done = last_evac
        seq += 1
    ev_wo_spill_all = ev_wo_spill[NKC - 1]

    t_h = tap("h", [TL, D], BF16)
    t_qln = tap("qln", [8, NKC, 128, 256], BF16)
    t_kt = tap("kt", [4, 128, TL], BF16)
    t_v = tap("v", [TL, 512], BF16)

    free_xg = [ev_modspill, ev_modspill]
    free_scr_a = ev_modspill
    free_hbf = [None, None]
    free_hT = None
    free_cs = ev_modspill
    free_qr = None
    free_qlnscr = None
    free_sqqk = None
    free_qtcol = [None, None, None, None]
    free_wpc = [None, None]
    free_bank = {2: None, 3: None, 4: None, 5: None}
    ev_hT_tile = {}
    ev_qspill = {}
    ev_ktile = {}
    ev_vtile = {}

    C_LN1 = (0, 8)
    C_LN1B = (8, 16)
    C_KV = (16, 32)
    C_Q = (32, 48)
    C_F = (48, 64)

    def ln_full(src_ap, n, eps, wait_src, cols, sq_dump, free_dump):
        c0 = cols[0]
        s_sum = stats[:, c0:c0 + 1]; s_sq = stats[:, c0 + 1:c0 + 2]
        s_mean = stats[:, c0 + 2:c0 + 3]; s_msq = stats[:, c0 + 3:c0 + 4]
        s_var = stats[:, c0 + 4:c0 + 5]; s_s = stats[:, c0 + 5:c0 + 6]
        s_r = stats[:, c0 + 6:c0 + 7]; s_nm = stats[:, c0 + 7:c0 + 8]
        e1 = b.op("vector", lambda e: e.reduce_sum(out=s_sum, in_=src_ap, axis=AX.X), waits=wait_src, inc=SV)
        def _sq(e):
            e.memzero(s_sq)
            return e.activation(out=sq_dump, in_=src_ap, func=AF.Square, accum_out=s_sq)
        e2 = b.op("scalar", _sq, waits=list(wait_src) + (free_dump if isinstance(free_dump, list) else [free_dump]), inc=SS)
        e3 = b.op("vector", lambda e: e.tensor_scalar_mul(out=s_mean, in0=s_sum, scalar1=1.0 / n), waits=[e1], inc=SV)
        e4 = b.op("vector", lambda e: e.tensor_scalar_mul(out=s_msq, in0=s_sq, scalar1=1.0 / n), waits=[e2], inc=SV)
        e5 = b.op("vector", lambda e: e.tensor_mul(out=s_var, in0=s_mean, in1=s_mean), waits=[e3], inc=SV)
        e6 = b.op("vector", lambda e: e.tensor_sub(out=s_var, in0=s_msq, in1=s_var), waits=[e4, e5], inc=SV)
        e7 = b.op("scalar", lambda e: e.activation(out=s_s, in_=s_var, func=AF.Sqrt, bias=f32(eps)), waits=[e6], inc=SS)
        e8 = b.op("vector", lambda e: e.reciprocal(out=s_r, in_=s_s), waits=[e7], inc=SV)
        e9 = b.op("vector", lambda e: e.tensor_scalar(out=s_nm, in0=s_mean, scalar1=s_r, scalar2=-1.0,
                                                      op0=ALU.mult, op1=ALU.mult), waits=[e8], inc=SV)
        return s_r, s_nm, e9, e2

    def ln_heads(src3, nh, hd, eps, wait_src, cols, scale_mul, sq_buf, free_sq, extra_sq_wait=()):
        c0 = cols[0]
        ssum = stats[:, c0:c0 + nh]; ssq = stats[:, c0 + nh:c0 + 2 * nh]
        smean = stats[:, c0 + 2 * nh:c0 + 3 * nh]; st1 = stats[:, c0 + 3 * nh:c0 + 4 * nh]
        su = stats[:, c0 + 4 * nh:c0 + 5 * nh]; ss_ = stats[:, c0 + 5 * nh:c0 + 6 * nh]
        sr = stats[:, c0 + 6 * nh:c0 + 7 * nh]
        src2 = src3.rearrange("p h c -> p (h c)")
        f1 = b.op("vector", lambda e: e.reduce_sum(out=ssum, in_=src3, axis=AX.X), waits=wait_src, inc=SV)
        f2 = b.op("scalar", lambda e: e.activation(out=sq_buf, in_=src2, func=AF.Square),
                  waits=list(wait_src) + [free_sq, f1] + list(extra_sq_wait), inc=SS)
        f3 = b.op("vector", lambda e: e.reduce_sum(out=ssq, in_=sq_buf.rearrange("p (h c) -> p h c", h=nh), axis=AX.X), waits=[f2], inc=SV)
        ft = b.op("scalar", lambda e: e.activation(out=st1, in_=ssum, func=AF.Square), waits=[f1], inc=SS)
        fm = b.op("scalar", lambda e: e.activation(out=smean, in_=ssum, func=AF.Identity, scale=f32(1.0 / hd)), waits=[f1], inc=SS)
        fu = b.op("vector", lambda e: e.scalar_tensor_tensor(out=su, in0=ssq, scalar=float(hd), in1=st1,
                                                             op0=ALU.mult, op1=ALU.subtract), waits=[f3, ft], inc=SV)
        fs = b.op("scalar", lambda e: e.activation(out=ss_, in_=su, func=AF.Sqrt, scale=f32(1.0 / (hd * hd)), bias=f32(eps)), waits=[fu], inc=SS)
        fr = b.op("vector", lambda e: e.reciprocal(out=sr, in_=ss_), waits=[fs], inc=SV)
        if scale_mul != 1.0:
            fr = b.op("vector", lambda e: e.tensor_scalar_mul(out=sr, in0=sr, scalar1=float(scale_mul)), waits=[fr], inc=SV)
        return smean, sr, [fr, fm], f3

    def rope(pv4, dst4, cosv, sinv, scr4, waits):
        e1 = b.op("vector", lambda e: e.tensor_tensor(out=dst4[:, :, 0], in0=pv4[:, :, 0], in1=cosv, op=ALU.mult), waits=waits, inc=SV)
        e2 = b.op("vector", lambda e: e.tensor_tensor(out=scr4[:, :, 0], in0=pv4[:, :, 1], in1=sinv, op=ALU.mult), waits=waits, inc=SV)
        e3 = b.op("vector", lambda e: e.tensor_sub(out=dst4[:, :, 0], in0=dst4[:, :, 0], in1=scr4[:, :, 0]), waits=[e1, e2], inc=SV)
        e4 = b.op("vector", lambda e: e.tensor_tensor(out=dst4[:, :, 1], in0=pv4[:, :, 1], in1=cosv, op=ALU.mult), waits=waits, inc=SV)
        e5 = b.op("vector", lambda e: e.tensor_tensor(out=scr4[:, :, 1], in0=pv4[:, :, 0], in1=sinv, op=ALU.mult), waits=waits, inc=SV)
        e6 = b.op("vector", lambda e: e.tensor_add(out=dst4[:, :, 1], in0=dst4[:, :, 1], in1=scr4[:, :, 1]), waits=[e4, e5], inc=SV)
        return [e3, e6]

    for g in range(5):
        ev_cs_c = b.dma("sync", cossin[:, :, 0:128],
                        cos_in[512 * g:512 * g + 512, :].rearrange("(tt p) c -> p tt c", p=128),
                        waits=[free_cs])
        ev_cs_s = b.dma("sync", cossin[:, :, 128:256],
                        sin_in[512 * g:512 * g + 512, :].rearrange("(tt p) c -> p tt c", p=128),
                        waits=[free_cs])
        ev_cs_both = [ev_cs_c, ev_cs_s]
        rope_evs = []
        group_consumers = []

        for ts in range(4):
            tt = 4 * g + ts
            xsl = tt % 2
            hsl = 0
            ev_x = b.dma("sync", xg[:, xsl], x_in[128 * tt:128 * tt + 128, :],
                         waits=[free_xg[xsl]])
            cols = C_LN1 if tt % 2 == 0 else C_LN1B
            s_r, s_nm, ev_st9, ev_sq = ln_full(xg[:, xsl], D, 1e-5, [ev_x], cols, scr_a, free_scr_a)
            ev_xn = b.op("scalar", lambda e, xsl=xsl, s_r=s_r, s_nm=s_nm: e.activation(
                out=scr_a, in_=xg[:, xsl], func=AF.Identity, scale=s_r, bias=s_nm),
                waits=[ev_st9], inc=SS)
            free_xg[xsl] = ev_xn
            ev_t1 = b.op("vector", lambda e: e.tensor_mul(out=scr_a, in0=scr_a, in1=modbc[:, 0]),
                 waits=[ev_xn, ev_bc_ca1], inc=SV)
            ev_h = b.op("vector", lambda e, hsl=hsl: e.tensor_add(out=h_bf[:, hsl], in0=scr_a, in1=modbc[:, 1]),
                        waits=[ev_t1, ev_bc_sa, free_hbf[hsl]], inc=SV)
            free_scr_a = ev_h
            if t_h is not None:
                b.dma("sync", t_h[128 * tt:128 * tt + 128, :], h_bf[:, hsl], waits=[ev_h])
            last_mm = None
            for half in range(2):
                stg = half
                mms = []
                for fi in range(8):
                    fc = 8 * half + fi
                    psl = bank_bf(6 + stg)[:, fi * 128:fi * 128 + 128]
                    mms.append(b.op("tensor",
                                    lambda e, psl=psl, hsl=hsl, fc=fc: e.transpose(psl, h_bf[:, hsl, 128 * fc:128 * fc + 128], ident),
                                    waits=[ev_h, ev_ident, free_stg[stg]], inc=ST))
                ev_ev = b.op("vector",
                             lambda e, stg=stg, half=half, ts=ts: e.tensor_copy(
                                 out=hT_sb[:, 8 * half:8 * half + 8, 128 * ts:128 * ts + 128],
                                 in_=bank_bf(6 + stg)[:, 0:1024].rearrange("p (f c) -> p f c", f=8)),
                             waits=[mms[-1], free_hT], inc=SV)
                free_stg[stg] = ev_ev
                last_mm = mms[-1]
            ev_hT_tile[tt] = ev_ev
            free_hbf[hsl] = last_mm

        ghT = [ev_hT_tile[4 * g + t_] for t_ in range(4)]

        for ts in range(4):
            tt = 4 * g + ts
            for which in ("k", "v"):
                bk = 2 + (ts % 2) * 2 + (0 if which == "k" else 1)
                wT = wkT_sb if which == "k" else wvT_sb
                def mmkv(e, ts=ts, wT=wT, bk=bk):
                    insts = []
                    for fc in range(NKC):
                        insts.append(e.matmul(bank(bk), hT_sb[:, fc, 128 * ts:128 * ts + 128],
                                              wT[:, fc, :], start=(fc == 0), stop=(fc == NKC - 1)))
                    return insts
                ev_mm = b.op("tensor", mmkv, waits=[ghT[ts], ev_wkv_done, free_bank[bk]], inc=ST)
                group_consumers.append(ev_mm)
                if which == "k":
                    pv4 = bank(bk).rearrange("p (hh t c) -> p hh t c", hh=2, t=2)
                    cosv = cossin[:, ts, 0:128].unsqueeze(1).broadcast_to([128, 2, 128])
                    sinv = cossin[:, ts, 128:256].unsqueeze(1).broadcast_to([128, 2, 128])
                    kr4 = q_r.rearrange("p (hh t c) -> p hh t c", hh=2, t=2)
                    scr4 = rope_scr.rearrange("p (hh t c) -> p hh t c", hh=2, t=2)
                    ev_rp = rope(pv4, kr4, cosv, sinv, scr4, [ev_mm, free_qr] + ev_cs_both)
                    rope_evs.append(ev_rp[-1])
                    free_bank[bk] = ev_rp[-1]
                    src3 = q_r.rearrange("p (hh c) -> p hh c", hh=2)
                    smean, sr, ev_ln, ev_sqf = ln_heads(src3, 2, 256, 1e-6, ev_rp, C_KV, 1.0, sq_qk, free_sqqk)
                    free_sqqk = ev_sqf
                    apls = []
                    for hh in range(2):
                        apls.append(b.op("vector", lambda e, hh=hh, src3=src3, smean=smean, sr=sr: e.tensor_scalar(
                            out=kln_bf[:, 256 * hh:256 * hh + 256], in0=src3[:, hh],
                            scalar1=smean[:, hh:hh + 1], scalar2=sr[:, hh:hh + 1],
                            op0=ALU.subtract, op1=ALU.mult),
                            waits=list(ev_ln) + [free_qlnscr], inc=SV))
                    free_qr = apls[-1]
                    mms = []
                    for dc in range(4):
                        psl = bank_bf(6)[:, dc * 128:dc * 128 + 128]
                        mms.append(b.op("tensor", lambda e, psl=psl, dc=dc: e.transpose(
                            psl, kln_bf[:, 128 * dc:128 * dc + 128], ident),
                            waits=[apls[-1], free_stg[0]], inc=ST))
                    ev_ke = b.op("vector", lambda e, tt=tt: e.tensor_copy(
                        out=kT_sb[:, :, 128 * tt:128 * tt + 128],
                        in_=bank_bf(6)[:, 0:512].rearrange("p (dc c) -> p dc c", dc=4)),
                        waits=[mms[-1]], inc=SV)
                    free_stg[0] = ev_ke
                    free_qlnscr = mms[-1]
                    ev_ktile[tt] = ev_ke
                else:
                    src3 = bank(bk).rearrange("p (hh c) -> p hh c", hh=2)
                    smean, sr, ev_ln, ev_sqf = ln_heads(src3, 2, 256, 1e-6, [ev_mm], C_KV, 1.0, sq_qk, free_sqqk)
                    free_sqqk = ev_sqf
                    apls = []
                    for hh in range(2):
                        apls.append(b.op("vector", lambda e, hh=hh, src3=src3, tt=tt, smean=smean, sr=sr: e.tensor_scalar(
                            out=v_sb[:, tt, 256 * hh:256 * hh + 256], in0=src3[:, hh],
                            scalar1=smean[:, hh:hh + 1], scalar2=sr[:, hh:hh + 1],
                            op0=ALU.subtract, op1=ALU.mult),
                            waits=list(ev_ln), inc=SV))
                    free_bank[bk] = apls[-1]
                    ev_vtile[tt] = apls[-1]
                    if t_v is not None:
                        b.dma("sync", t_v[128 * tt:128 * tt + 128, :], v_sb[:, tt, :], waits=[apls[-1]])

        if g >= 1:
            for j in range(4):
                wsl = j % 2
                ev_wp = b.dma("gpsimd", wpc[:, wsl],
                              wqT_hbm[:, :, 512 * j:512 * j + 512].rearrange("fc p c -> p fc c"),
                              waits=[ev_wq_spill, free_wpc[wsl]])
                for ts in range(4):
                    tt = 4 * g + ts
                    bk = 2 + ((4 * j + ts) % 4)
                    def mmq(e, ts=ts, wsl=wsl, bk=bk):
                        insts = []
                        for fc in range(NKC):
                            insts.append(e.matmul(bank(bk), hT_sb[:, fc, 128 * ts:128 * ts + 128],
                                                  wpc[:, wsl, fc, :], start=(fc == 0), stop=(fc == NKC - 1)))
                        return insts
                    ev_mm = b.op("tensor", mmq, waits=[ev_wp, ghT[ts], free_bank[bk]], inc=ST)
                    group_consumers.append(ev_mm)
                    free_wpc[wsl] = ev_mm
                    pv4 = bank(bk).rearrange("p (hh t c) -> p hh t c", hh=2, t=2)
                    cosv = cossin[:, ts, 0:128].unsqueeze(1).broadcast_to([128, 2, 128])
                    sinv = cossin[:, ts, 128:256].unsqueeze(1).broadcast_to([128, 2, 128])
                    qr4 = q_r.rearrange("p (hh t c) -> p hh t c", hh=2, t=2)
                    scr4 = rope_scr.rearrange("p (hh t c) -> p hh t c", hh=2, t=2)
                    ev_rp = rope(pv4, qr4, cosv, sinv, scr4, [ev_mm, free_qr] + ev_cs_both)
                    rope_evs.append(ev_rp[-1])
                    free_bank[bk] = ev_rp[-1]
                    src3 = q_r.rearrange("p (hh c) -> p hh c", hh=2)
                    smean, sr, ev_ln, ev_sqf = ln_heads(src3, 2, 256, 1e-6, ev_rp, C_Q, 16.0, sq_qk, free_sqqk)
                    free_sqqk = ev_sqf
                    apls = []
                    for hh in range(2):
                        apls.append(b.op("vector", lambda e, hh=hh, src3=src3, smean=smean, sr=sr: e.tensor_scalar(
                            out=qln_scr[:, 256 * hh:256 * hh + 256], in0=src3[:, hh],
                            scalar1=smean[:, hh:hh + 1], scalar2=sr[:, hh:hh + 1],
                            op0=ALU.subtract, op1=ALU.mult),
                            waits=list(ev_ln) + [free_qlnscr], inc=SV))
                    free_qr = apls[-1]
                    mms = []
                    for dc in range(4):
                        psl = bank_bf(7)[:, 512 + dc * 128:512 + dc * 128 + 128]
                        mms.append(b.op("tensor", lambda e, psl=psl, dc=dc: e.transpose(
                            psl, qln_scr[:, 128 * dc:128 * dc + 128], ident),
                            waits=[apls[-1], free_stg[1]], inc=ST))
                    ev_qe = b.op("vector", lambda e, j=j, ts=ts: e.tensor_copy(
                        out=qt_stage[:, 4 * j:4 * j + 4, 128 * ts:128 * ts + 128],
                        in_=bank_bf(7)[:, 512:1024].rearrange("p (dc c) -> p dc c", dc=4)),
                        waits=[mms[-1], free_qtcol[ts]], inc=SV)
                    free_stg[1] = ev_qe
                    free_qlnscr = mms[-1]
                    if ts == 3 and j == 3:
                        pass
                    ev_hT_tile[("q", g, j, ts)] = ev_qe
            for half in range(2):
                qi = 2 * (g - 1) + half
                done = [ev_hT_tile[("q", g, j, 2 * half + s_)] for j in range(4) for s_ in range(2)]
                ev_qspill[qi] = b.dma("sync", qt_hbm[qi].rearrange("fc p c -> p fc c"),
                                      qt_stage[:, :, 256 * half:256 * half + 256], waits=done)
                for s_ in range(2):
                    free_qtcol[2 * half + s_] = ev_qspill[qi]
                if t_qln is not None:
                    b.dma("sync", t_qln[qi], qt_hbm[qi], waits=[ev_qspill[qi]])
        free_hT = group_consumers[-1]
        if rope_evs:
            free_cs = rope_evs[-1]

    if t_kt is not None:
        b.dma("sync", t_kt.rearrange("dc p c -> p dc c"), kT_sb[:],
              waits=[ev_ktile[NT - 1]])

    fenceA1 = b.fence()
    cursor[0] = PB
    qtb = buf(16384, BF16, [2, NKC, 256])
    mkb = buf(6144, BF16, [2, 6, 256])
    s_t = buf(6144, F32, [1536])
    ptb = buf(6144, BF16, [2, 1536])
    o_sb = buf(16384, BF16, [4, 2048])
    otb = buf(16384, BF16, [NKC, 512])
    x1t = buf(32768, F32, [4, 2048])
    rc_sb = buf(64, F32, [16])
    wpc2 = buf(32768, BF16, [2, NKC, 512])
    A2_END = cursor[0]
    assert A2_END <= ARENA

    t_x1 = tap("x1", [L, D])
    t_osb = tap("osb", [L, D], BF16)
    ev_bc_ga = bc_mod(0, 2, [fenceA1[3]])

    free_qtb = [fenceA1[4], fenceA1[4]]
    free_mkb = [fenceA1[4], fenceA1[4]]
    free_st = fenceA1[2]
    free_ptb = [None, None]
    free_Sreg = [fenceA1[4], fenceA1[4]]
    free_obank = fenceA1[4]
    free_rsbank = fenceA1[4]
    free_osb = [None, None, None, None]
    free_otb = None
    free_x1t = [None, None, None, None]
    free_wpc2 = [fenceA1[4], fenceA1[4]]
    ev_x1sp = {}

    b.wait_only("sync", [e_ for e_ in fenceA1 if e_ is not None])
    b.wait_only("gpsimd", [e_ for e_ in fenceA1 if e_ is not None])
    b.wait_only("tensor", [e_ for e_ in fenceA1 if e_ is not None])
    b.wait_only("scalar", [e_ for e_ in fenceA1 if e_ is not None])
    b.wait_only("vector", [e_ for e_ in fenceA1 if e_ is not None])

    blocks = [(qi, h) for qi in range(8) for h in range(8)]
    ev_tanh = {}
    ev_qtb = {}
    ev_mkb = {}
    for qi in range(8):
        sl = qi % 2
        ev_qtb[qi] = b.dma("sync", qtb[:, sl], qt_hbm[qi].rearrange("fc p c -> p fc c"),
                           waits=[ev_qspill[qi], free_qtb[sl]])
        ev_mkb[qi] = b.dma("sync", mkb[:, sl], masks_in[qi].rearrange("j p c -> p j c"),
                           waits=[free_mkb[sl]])
        kneed = min(2 * qi + 5, NT - 1)
        ev_kv_need = [ev_ktile[kneed], ev_vtile[kneed]]
        for h in range(8):
            bi = 8 * qi + h
            r = bi % 2
            kv = h // 4
            def smm(e, qi=qi, h=h, r=r, kv=kv, sl=sl):
                insts = []
                for j in range(6):
                    dst = pool[:, 1536 * r + 256 * j:1536 * r + 256 * j + 256]
                    insts.append(e.matmul(dst, ident, mkb[:, sl, j], start=True, stop=False))
                    for dc in range(2):
                        insts.append(e.matmul(dst, kT_sb[:, 2 * kv + dc, 256 * qi + 128 * j:256 * qi + 128 * j + 128],
                                              qtb[:, sl, 2 * h + dc, :], start=False, stop=(dc == 1)))
                return insts
            ev_smm = b.op("tensor", smm,
                          waits=[ev_qtb[qi], ev_mkb[qi], free_Sreg[r], ev_ident] + ev_kv_need, inc=ST)
            ev_th = b.op("scalar", lambda e, r=r: e.activation(
                out=s_t, in_=pool[:, 1536 * r:1536 * r + 1536], func=AF.Tanh, scale=f32(1.0 / 30.0)),
                waits=[ev_smm, free_st], inc=SS)
            free_Sreg[r] = ev_th
            ev_tanh[bi] = ev_th
            psl = bi % 2
            ev_exp = b.op("scalar", lambda e, psl=psl: e.activation(
                out=ptb[:, psl], in_=s_t, func=AF.Exp, scale=f32(30.0), bias=f32(-30.0)),
                waits=[free_ptb[psl]], inc=SS)
            free_st = ev_exp
            def rsmm(e, psl=psl):
                insts = []
                for qs in range(2):
                    dstc = pool[:, 3584 + qs:3584 + qs + 1]
                    for j in range(6):
                        insts.append(e.matmul(dstc, ptb[:, psl, 256 * j + 128 * qs:256 * j + 128 * qs + 128],
                                              ones_col, start=(j == 0), stop=(j == 5)))
                return insts
            ev_rs = b.op("tensor", rsmm, waits=[ev_exp, free_rsbank], inc=ST)
            ev_rc = b.op("vector", lambda e: e.reciprocal(out=rc_sb[:, 0:2], in_=pool[:, 3584:3586]),
                         waits=[ev_rs], inc=SV)
            free_rsbank = ev_rc
            def pvmm(e, qi=qi, h=h, kv=kv, psl=psl):
                insts = []
                for qs in range(2):
                    dst = pool[:, 3072 + 256 * qs:3072 + 256 * qs + 256]
                    for j in range(6):
                        insts.append(e.matmul(dst, ptb[:, psl, 256 * j + 128 * qs:256 * j + 128 * qs + 128],
                                              v_sb[:, 2 * qi + j, 256 * kv:256 * kv + 256],
                                              start=(j == 0), stop=(j == 5)))
                return insts
            ev_pv = b.op("tensor", pvmm, waits=[ev_exp, free_obank], inc=ST)
            free_ptb[psl] = ev_pv
            evs = []
            for qs in range(2):
                osl = 2 * (qi % 2) + qs
                evs.append(b.op("scalar", lambda e, qs=qs, h=h, osl=osl: e.activation(
                    out=o_sb[:, osl, 256 * h:256 * h + 256], in_=pool[:, 3072 + 256 * qs:3072 + 256 * qs + 256],
                    func=AF.Identity, scale=rc_sb[:, qs:qs + 1], bias=f32(0.0)),
                    waits=[ev_pv, ev_rc, free_osb[osl]], inc=SS))
            free_obank = evs[-1]
            ev_tanh[("oev", bi)] = evs[-1]

        if qi % 2 == 1:
            grp = qi // 2
            oev = [ev_tanh[("oev", 8 * (2 * grp) + hh)] for hh in range(8)] + \
                  [ev_tanh[("oev", 8 * (2 * grp + 1) + hh)] for hh in range(8)]
            if t_osb is not None:
                b.dma("sync", t_osb[512 * grp:512 * grp + 512, :].rearrange("(ts p) d -> p ts d", p=128),
                      o_sb[:], waits=oev)
            ev_ot = {}
            for tsub in range(4):
                for half in range(2):
                    stgb = half
                    mms = []
                    for fi in range(8):
                        oc = 8 * half + fi
                        psl = pool_bf[:, stgb * 1024 + fi * 128: stgb * 1024 + fi * 128 + 128]
                        w_ = oev + [free_Sreg[0], free_Sreg[1], free_otb]
                        mms.append(b.op("tensor", lambda e, psl=psl, tsub=tsub, oc=oc: e.transpose(
                            psl, o_sb[:, tsub, 128 * oc:128 * oc + 128], ident),
                            waits=w_ + ([ev_ot[(tsub, half, "e")]] if (tsub, half, "e") in ev_ot else
                                        ([ev_ot[(tsub - 1, half, "e")]] if (tsub - 1, half, "e") in ev_ot else [])),
                            inc=ST))
                    ev_ot[(tsub, half)] = mms[-1]
                    ev_ot[(tsub, half, "e")] = b.op("vector",
                        lambda e, stgb=stgb, half=half, tsub=tsub: e.tensor_copy(
                            out=otb[:, 8 * half:8 * half + 8, 128 * tsub:128 * tsub + 128],
                            in_=pool_bf[:, stgb * 1024:stgb * 1024 + 1024].rearrange("p (f c) -> p f c", f=8)),
                        waits=[mms[-1]], inc=SV)
            ev_ot_all = [ev_ot[(tsub, half, "e")] for tsub in range(4) for half in range(2)]
            free_Sreg[0] = ev_ot_all[-1]
            free_Sreg[1] = ev_ot_all[-1]
            free_osb = [ev_ot[(tsub, 1)] for tsub in range(4)]
            for j in range(4):
                wsl = j % 2
                ev_wp = b.dma("gpsimd", wpc2[:, wsl],
                              woT_hbm[:, :, 512 * j:512 * j + 512].rearrange("fc p c -> p fc c"),
                              waits=[ev_wo_spill_all, free_wpc2[wsl]])
                for tsub in range(4):
                    bk = 4 + ((4 * j + tsub) % 2)
                    def mmo(e, tsub=tsub, wsl=wsl, bk=bk):
                        insts = []
                        for oc in range(NKC):
                            insts.append(e.matmul(bank(bk), otb[:, oc, 128 * tsub:128 * tsub + 128],
                                                  wpc2[:, wsl, oc, :], start=(oc == 0), stop=(oc == NKC - 1)))
                        return insts
                    ev_mo = b.op("tensor", mmo, waits=[ev_wp, ev_ot_all[-1],
                                                       free_bank.get(("a2", bk))], inc=ST)
                    free_wpc2[wsl] = ev_mo
                    def resid(e, bk=bk, j=j, tsub=tsub):
                        return e.tensor_mul(out=x1t[:, tsub, 512 * j:512 * j + 512],
                                            in0=bank(bk), in1=modbc[:, 0][:, 512 * j:512 * j + 512])
                    ev_r1 = b.op("vector", resid, waits=[ev_mo, ev_bc_ga, free_x1t[tsub]], inc=SV)
                    free_bank[("a2", bk)] = ev_r1
                    if j == 3:
                        tloc = 512 * grp + 128 * tsub
                        evsp = b.dma("sync", x1_hbm[tloc:tloc + 128, :], x1t[:, tsub],
                                     waits=[ev_r1])
                        ev_x1sp[(grp, tsub)] = evsp
                        free_x1t[tsub] = evsp
            free_otb = free_wpc2[(3) % 2]
            free_Sreg[0] = free_bank[("a2", 4)]
            free_Sreg[1] = free_bank[("a2", 5)]
        free_qtb[sl] = ev_smm
        free_mkb[sl] = ev_smm
    fenceA2 = b.fence()
    cursor[0] = PERS_END
    h2T = buf(65536, BF16, [NKC, 2048])
    wgu = buf(32768, BF16, [2, NKC, 512])
    cw16 = buf(512, F32, [16, 8])
    REGA = cursor[0]
    xg3 = buf(16384, F32, [2, 2048])
    scr3 = buf(8192, F32, [2048])
    h_bf3 = buf(4096, BF16, [2048])
    rwb = buf(4096, BF16, [2048])
    rwT = buf(256, BF16, [NKC, 8])
    REGA_END = cursor[0]
    cursor[0] = REGA
    a_bf = buf(32768, BF16, [8, 2048])
    wd = buf(32768, BF16, [4, 8, 512])
    sil = buf(8192, F32, [4, 512])
    ystage = buf(16384, F32, [2, 2048])
    F_END = max(cursor[0], REGA_END)
    assert F_END <= ARENA, F_END
    r8f = av[:, REGA:REGA + 8192].bitcast(F32)
    nbf = av[:, REGA + 8192:REGA + 16384].bitcast(F32)

    t_cw = tap("cw", [L, 8])
    t_h2 = tap("h2", [L, D], BF16)

    b.wait_only("sync", [e_ for e_ in fenceA2 if e_ is not None])
    b.wait_only("gpsimd", [e_ for e_ in fenceA2 if e_ is not None])
    b.wait_only("tensor", [e_ for e_ in fenceA2 if e_ is not None])
    b.wait_only("scalar", [e_ for e_ in fenceA2 if e_ is not None])
    b.wait_only("vector", [e_ for e_ in fenceA2 if e_ is not None])

    ev_r8 = b.dma("sync", r8f[0:8, :], router_in)
    ev_nb = b.dma("sync", nbf[0:8, :], normw_in.broadcast_to([8, D]))
    ev_rw = b.op("vector", lambda e: e.tensor_mul(out=r8f[0:8, :], in0=r8f[0:8, :], in1=nbf[0:8, :]),
                 waits=[ev_r8, ev_nb], inc=SV)
    ev_rwb = b.op("vector", lambda e: e.tensor_copy(out=rwb[0:8, :], in_=r8f[0:8, :]),
                  waits=[ev_rw], inc=SV)
    mmsr = []
    for fc in range(NKC):
        psl = pool_bf[:, 6 * 1024 + fc * 8: 6 * 1024 + fc * 8 + 8]
        mmsr.append(b.op("tensor", lambda e, psl=psl, fc=fc: e.transpose(
            psl, rwb[0:8, 128 * fc:128 * fc + 128], ident[0:8, 0:8]),
            waits=[ev_rwb], inc=ST))
    ev_rwT = b.op("vector", lambda e: e.tensor_copy(
        out=rwT[:], in_=pool_bf[:, 6 * 1024:6 * 1024 + 128].rearrange("p (fc x) -> p fc x", fc=16)),
        waits=[mmsr[-1]], inc=SV)

    ev_bc_cb1 = bc_mod(0, 4, [fenceA2[3]])
    ev_bc_sb = bc_mod(1, 3, [fenceA2[3]])

    free_xg3 = [ev_rwb, ev_rwb]
    free_scr3 = ev_nb
    free_hbf3 = None
    free_stgF = [ev_rwT, fenceA2[3]]
    free_logitb = ev_rwT
    free_h2Tcol = None
    ev_h2T_all = None
    ev_cw_all = []
    ev_rms = {}
    SCALE_CW = 4.0

    for fg in range(4):
        ev_h2T_t = {}
        free_logit = None
        free_d12 = None
        ev_cw = {}
        for ts in range(4):
            gt = 4 * fg + ts
            tl = 128 * gt
            xsl = ts % 2
            ev_x1 = b.dma("sync", xg3[:, xsl], x1_hbm[tl:tl + 128, :],
                          waits=[ev_x1sp[(fg, ts)], free_xg3[xsl]])
            ev_xr = b.dma("sync", scr3, x_in[512 + tl:512 + tl + 128, :],
                          waits=[free_scr3])
            ev_add = b.op("vector", lambda e, xsl=xsl: e.tensor_add(out=xg3[:, xsl], in0=xg3[:, xsl], in1=scr3),
                          waits=[ev_x1, ev_xr], inc=SV)
            if t_x1 is not None:
                b.dma("sync", t_x1[tl:tl + 128, :], xg3[:, xsl], waits=[ev_add])
            cols = C_LN1 if ts % 2 == 0 else C_LN1B
            s_r, s_nm, ev_st9, _ = ln_full(xg3[:, xsl], D, 1e-5, [ev_add], cols, scr3, [ev_add])
            ev_xn = b.op("scalar", lambda e, xsl=xsl, s_r=s_r, s_nm=s_nm: e.activation(
                out=scr3, in_=xg3[:, xsl], func=AF.Identity, scale=s_r, bias=s_nm),
                waits=[ev_st9], inc=SS)
            ev_m1 = b.op("vector", lambda e: e.tensor_mul(out=scr3, in0=scr3, in1=modbc[:, 0]),
                 waits=[ev_xn, ev_bc_cb1], inc=SV)
            ev_h2 = b.op("vector", lambda e: e.tensor_add(out=scr3, in0=scr3, in1=modbc[:, 1]),
                         waits=[ev_m1, ev_bc_sb], inc=SV)
            c0 = C_F[0]
            s_sq = stats[:, c0 + 8 + ts:c0 + 9 + ts]
            def _sqr(e, s_sq=s_sq, xsl=xsl):
                e.memzero(s_sq)
                return e.activation(out=xg3[:, xsl], in_=scr3, func=AF.Square, accum_out=s_sq)
            ev_sq = b.op("scalar", _sqr, waits=[ev_h2], inc=SS)
            free_xg3[xsl] = ev_sq
            s_rms = stats[:, c0 + ts:c0 + ts + 1]
            ev_rm1 = b.op("vector", lambda e, s_sq=s_sq, s_rms=s_rms: e.tensor_scalar_mul(out=s_rms, in0=s_sq, scalar1=1.0 / D), waits=[ev_sq], inc=SV)
            ev_sq2 = b.op("scalar", lambda e, s_rms=s_rms: e.activation(out=s_rms, in_=s_rms, func=AF.Sqrt, bias=f32(1e-6)),
                          waits=[ev_rm1], inc=SS)
            ev_rms[ts] = b.op("vector", lambda e, s_rms=s_rms: e.reciprocal(out=s_rms, in_=s_rms), waits=[ev_sq2], inc=SV)
            ev_hb = b.op("vector", lambda e: e.tensor_copy(out=h_bf3, in_=scr3), waits=[ev_h2, free_hbf3], inc=SV)
            free_scr3 = ev_hb
            if t_h2 is not None:
                b.dma("sync", t_h2[tl:tl + 128, :], h_bf3, waits=[ev_hb])
            last_mm = None
            for half in range(2):
                stg = half
                mms = []
                for fi in range(8):
                    fc = 8 * half + fi
                    psl = bank_bf(6 + stg)[:, fi * 128:fi * 128 + 128]
                    mms.append(b.op("tensor",
                                    lambda e, psl=psl, fc=fc: e.transpose(psl, h_bf3[:, 128 * fc:128 * fc + 128], ident),
                                    waits=[ev_hb, free_stgF[stg]], inc=ST))
                ev_ev = b.op("vector",
                             lambda e, stg=stg, half=half, gt=gt: e.tensor_copy(
                                 out=h2T[:, 8 * half:8 * half + 8, 128 * gt:128 * gt + 128],
                                 in_=bank_bf(6 + stg)[:, 0:1024].rearrange("p (f c) -> p f c", f=8)),
                             waits=[mms[-1], free_h2Tcol], inc=SV)
                free_stgF[stg] = ev_ev
                last_mm = mms[-1]
            free_hbf3 = last_mm
            ev_h2T_t[ts] = ev_ev
        ev_h2T_all = ev_h2T_t[3]
        for ts in range(4):
            gt = 4 * fg + ts
            def rmm(e, gt=gt):
                insts = []
                for fc in range(NKC):
                    insts.append(e.matmul(pool[:, 3840:3848],
                                          h2T[:, fc, 128 * gt:128 * gt + 128],
                                          rwT[:, fc, :], start=(fc == 0), stop=(fc == NKC - 1)))
                return insts
            ev_lmm = b.op("tensor", rmm, waits=[ev_h2T_t[3], ev_rwT, free_logitb], inc=ST)
            c0 = C_F[0]
            s_rms = stats[:, c0 + ts:c0 + ts + 1]
            ev_lg = b.op("scalar", lambda e, s_rms=s_rms: e.activation(
                out=logit_sb, in_=pool[:, 3840:3848], func=AF.Identity, scale=s_rms, bias=f32(0.0)),
                waits=[ev_lmm, ev_rms[ts], free_logit], inc=SS)
            free_logitb = ev_lg
            ev_m8 = b.op("vector", lambda e: e.max(out=srt8, in_=logit_sb), waits=[ev_lg], inc=SV)
            d12 = stats[:, c0 + 12:c0 + 13]
            w1 = stats[:, c0 + 13:c0 + 14]
            w2 = stats[:, c0 + 14:c0 + 15]
            r1 = b.op("vector", lambda e, d12=d12: e.tensor_sub(out=d12, in0=srt8[:, 0:1], in1=srt8[:, 1:2]), waits=[ev_m8, free_d12], inc=SV)
            ev_tw = b.op("scalar", lambda e, d12=d12, w1=w1: e.activation(out=w1, in_=d12, func=AF.Tanh, scale=f32(0.5)),
                         waits=[r1], inc=SS)
            r2 = b.op("vector", lambda e, w1=w1: e.tensor_scalar(out=w1, in0=w1, scalar1=0.5 * SCALE_CW, scalar2=0.5 * SCALE_CW, op0=ALU.mult, op1=ALU.add), waits=[ev_tw], inc=SV)
            r3 = b.op("vector", lambda e, w1=w1, w2=w2: e.tensor_scalar(out=w2, in0=w1, scalar1=-1.0, scalar2=SCALE_CW, op0=ALU.mult, op1=ALU.add), waits=[r2], inc=SV)
            m1 = stats[:, 72:80]
            r4 = b.op("vector", lambda e, m1=m1: e.tensor_single_scalar(out=m1, in_=logit_sb, scalar=srt8[:, 0:1], op=ALU.is_ge), waits=[ev_m8], inc=SV)
            m2 = cw16[:, gt]
            r5 = b.op("vector", lambda e, m2=m2: e.tensor_single_scalar(out=m2, in_=logit_sb, scalar=srt8[:, 1:2], op=ALU.is_ge), waits=[ev_m8], inc=SV)
            r6 = b.op("vector", lambda e, m1=m1, m2=m2: e.tensor_sub(out=m2, in0=m2, in1=m1), waits=[r4, r5], inc=SV)
            r7 = b.op("vector", lambda e, m1=m1, w1=w1: e.tensor_scalar_mul(out=m1, in0=m1, scalar1=w1), waits=[r2, r6], inc=SV)
            ev_cw_all.append(b.op("vector", lambda e, m1=m1, m2=m2, w2=w2: e.scalar_tensor_tensor(
                out=m2, in0=m2, scalar=w2, in1=m1, op0=ALU.mult, op1=ALU.add), waits=[r3, r7], inc=SV))
            free_logit = r5
            free_d12 = ev_tw
        free_stgF[1] = free_logitb
        if t_cw is not None:
            b.dma("sync", t_cw[512 * fg:512 * fg + 512, :].rearrange("(ts p) x -> p ts x", p=128),
                  cw16[:, 4 * fg:4 * fg + 4], waits=[ev_cw_all[-1]])
        free_h2Tcol = free_h2Tcol

    ev_cw_last = ev_cw_all[-1]
    fenceSPA = b.fence()

    free_wgu = [fenceSPA[4], fenceSPA[4]]
    free_wd = [fenceSPA[1] for _ in range(4)]
    free_gbank = {i: fenceSPA[3] for i in range(8)}
    free_sil = fenceSPA[2]
    free_abf = {f_: fenceSPA[4] for f_ in range(8)}
    free_ystage = [fenceSPA[1], fenceSPA[1]]
    y_tile_chain = {t_: None for t_ in range(16)}

    for e_i in range(E):
        ev_load = {}
        ev_load[("g", 0)] = b.dma("gpsimd", wgu[:, 0],
                                  gate_in[e_i, :, 0:512].rearrange("(dc p) f -> p dc f", p=128),
                                  waits=[free_wgu[0], fenceSPA[3], fenceSPA[2]])
        ev_load[("u", 0)] = b.dma("gpsimd", wgu[:, 1],
                                  up_in[e_i, :, 0:512].rearrange("(dc p) f -> p dc f", p=128),
                                  waits=[free_wgu[1]])
        last_gmm = None
        last_umm = None
        last_att = None
        for hf_ in range(2):
            if hf_ == 1:
                ev_load[("g", 1)] = b.dma("gpsimd", wgu[:, 0],
                                          gate_in[e_i, :, 512:1024].rearrange("(dc p) f -> p dc f", p=128),
                                          waits=[last_gmm])
                ev_load[("u", 1)] = b.dma("gpsimd", wgu[:, 1],
                                          up_in[e_i, :, 512:1024].rearrange("(dc p) f -> p dc f", p=128),
                                          waits=[last_umm])
            for fq in range(4):
                fcp = 4 * hf_ + fq
                def gmm(e, fq=fq):
                    insts = []
                    for dc in range(NKC):
                        for tp in range(4):
                            insts.append(e.matmul(bank(tp), wgu[:, 0, dc, 128 * fq:128 * fq + 128],
                                                  h2T[:, dc, 512 * tp:512 * tp + 512],
                                                  start=(dc == 0), stop=(dc == NKC - 1)))
                    return insts
                ev_gm = b.op("tensor", gmm, waits=[ev_load[("g", hf_)], ev_h2T_all] +
                             [free_gbank[tp] for tp in range(4)], inc=ST)
                last_gmm = ev_gm
                ev_sls = []
                for tp in range(4):
                    ev_sls.append(b.op("scalar", lambda e, tp=tp: e.activation(out=sil[:, tp], in_=bank(tp), func=AF.Silu),
                                       waits=[ev_gm, free_sil], inc=SS))
                def umm(e, fq=fq):
                    insts = []
                    for dc in range(NKC):
                        for tp in range(4):
                            insts.append(e.matmul(bank(4 + tp), wgu[:, 1, dc, 128 * fq:128 * fq + 128],
                                                  h2T[:, dc, 512 * tp:512 * tp + 512],
                                                  start=(dc == 0), stop=(dc == NKC - 1)))
                    return insts
                ev_um = b.op("tensor", umm, waits=[ev_load[("u", hf_)], fenceSPA[2]] +
                             [free_gbank[4 + tp] for tp in range(4)], inc=ST)
                last_umm = ev_um
                ev_ats = []
                for tp in range(4):
                    ev_ats.append(b.op("vector", lambda e, tp=tp, fcp=fcp: e.tensor_mul(
                        out=a_bf[:, fcp, 512 * tp:512 * tp + 512], in0=sil[:, tp], in1=bank(4 + tp)),
                        waits=[ev_sls[tp], ev_um, free_abf[fcp]], inc=SV))
                for tp in range(4):
                    free_gbank[tp] = ev_sls[tp]
                    free_gbank[4 + tp] = ev_ats[tp]
                free_sil = ev_ats[-1]
                last_att = ev_ats[-1]
                free_abf[fcp] = None
        for dp in range(4):
            ev_load[("d", dp)] = b.dma("gpsimd", wd[:, dp],
                                       down_in[e_i, :, 512 * dp:512 * dp + 512].rearrange("(fc p) d -> p fc d", p=128),
                                       waits=[free_wd[dp]])
        last_eo = {}
        for tsub in range(16):
            par = tsub % 2
            bks = [4 * par + i for i in range(4)]
            def emm(e, tsub=tsub, bks=bks):
                insts = []
                for fcp in range(8):
                    for di, dp in enumerate(range(4)):
                        insts.append(e.matmul(bank(bks[di]), a_bf[:, fcp, 128 * tsub:128 * tsub + 128],
                                              wd[:, dp, fcp, :], start=(fcp == 0), stop=(fcp == 7)))
                return insts
            ev_em = b.op("tensor", emm, waits=[last_att] + [ev_load[("d", dp)] for dp in range(4)] +
                         [free_gbank[bk] for bk in bks], inc=ST)
            ysl = tsub % 2
            evacs = []
            for di in range(4):
                evacs.append(b.op("scalar", lambda e, di=di, bks=bks, ysl=ysl, tsub=tsub, e_i=e_i: e.activation(
                    out=ystage[:, ysl, 512 * di:512 * di + 512], in_=bank(bks[di]),
                    func=AF.Identity, scale=cw16[:, tsub, e_i:e_i + 1], bias=f32(0.0)),
                    waits=[ev_em, ev_cw_last, free_ystage[ysl]], inc=SS))
            for di in range(4):
                free_gbank[bks[di]] = evacs[di]
            evacc = b.dma("gpsimd", y_hbm[128 * tsub:128 * tsub + 128, :], ystage[:, ysl],
                          waits=[evacs[-1], y_tile_chain[tsub]], accum=(e_i > 0))
            y_tile_chain[tsub] = evacc
            free_ystage[ysl] = evacc
            last_eo[tsub] = ev_em
        for dp in range(4):
            free_wd[dp] = last_eo[15]
        free_wgu[0] = last_gmm
        free_wgu[1] = last_umm
        for f_ in range(8):
            free_abf[f_] = last_eo[15]

    ev_bc_gb = bc_mod(1, 5, [fenceSPA[3]])
    free_oxg = [fenceSPA[0], fenceSPA[0]]
    free_oscr = fenceSPA[0]
    for tsub in range(16):
        tl = 128 * tsub
        ysl = tsub % 2
        ev_y = b.dma("sync", ystage[:, ysl], y_hbm[tl:tl + 128, :],
                     waits=[y_tile_chain[tsub], free_ystage[ysl]])
        xsl = tsub % 2
        ev_x2 = b.dma("sync", xg3[:, xsl], x_in[512 + tl:512 + tl + 128, :],
                      waits=[free_oxg[xsl]])
        ev_sp = b.dma("sync", scr3, x1_hbm[tl:tl + 128, :], waits=[free_oscr])
        ev_o1 = b.op("vector", lambda e, ysl=ysl: e.tensor_mul(
            out=ystage[:, ysl], in0=ystage[:, ysl], in1=modbc[:, 1]),
            waits=[ev_y, ev_bc_gb], inc=SV)
        ev_o2 = b.op("vector", lambda e, ysl=ysl, xsl=xsl: e.tensor_add(
            out=ystage[:, ysl], in0=ystage[:, ysl], in1=xg3[:, xsl]),
            waits=[ev_o1, ev_x2], inc=SV)
        ev_o3 = b.op("vector", lambda e, ysl=ysl: e.tensor_add(
            out=ystage[:, ysl], in0=ystage[:, ysl], in1=scr3),
            waits=[ev_o2, ev_sp], inc=SV)
        free_oxg[xsl] = ev_o2
        free_oscr = ev_o3
        ev_st_ = b.dma("sync", out[tl:tl + 128, :], ystage[:, ysl], waits=[ev_o3])
        free_ystage[ysl] = ev_st_

    b.wait_only("gpsimd", [(QS, b.counts[QS], "sync"), (QG, b.counts[QG], "gpsimd")])
    b.emit()
    return nc, tap_aps


def make_consts():
    ident = np.eye(128, dtype=ml_dtypes.bfloat16)
    return ident


def stage_core(inputs, b_, hf):
    S = 4096
    t0 = hf * 2048
    x = np.asarray(inputs["L0_attn_hidden_states"][b_])
    xl = np.zeros((TL, D), np.float32)
    lo = t0 - 512
    if lo < 0:
        xl[512:] = x[t0:t0 + 2048]
    else:
        xl[:] = x[lo:t0 + 2048]
    pos = np.arange(t0 - 512, t0 + 2048, dtype=np.float64)
    inv = 1.0 / (1e6 ** (np.arange(0, 256, 2, dtype=np.float64) / 256.0))
    fr = pos[:, None] * inv[None, :]
    cos_t = np.cos(fr).astype(np.float32)
    sin_t = np.sin(fr).astype(np.float32)
    masks = np.zeros((8, 6, 128, 256), np.float32)
    for qi in range(8):
        for j in range(6):
            r = np.arange(128)[:, None]
            c = np.arange(256)[None, :]
            key_g = t0 - 512 + 256 * qi + 128 * j + r
            qry_g = t0 + 256 * qi + c
            allowed = (key_g <= qry_g) & (key_g > qry_g - 512) & (key_g >= 0)
            masks[qi, j] = np.where(allowed, 0.0, -32768.0)
    return dict(
        x_in=xl,
        wq=np.asarray(inputs["L0_attn_Wq"]),
        wk=np.asarray(inputs["L0_attn_Wk"]),
        wv=np.asarray(inputs["L0_attn_Wv"]),
        wo=np.asarray(inputs["L0_attn_Wo"]),
        router=np.asarray(inputs["L0_ffn_router_w"]),
        normw=np.asarray(inputs["L0_ffn_norm_w"]).reshape(1, D),
        gate=np.asarray(inputs["L0_ffn_gate_w"]),
        up=np.asarray(inputs["L0_ffn_up_w"]),
        down=np.asarray(inputs["L0_ffn_down_w"]),
        c_row=np.asarray(inputs["L0_c"][b_]),
        wmod=np.asarray(inputs["L0_wmod"]),
        cos_in=cos_t, sin_in=sin_t,
        masks_in=masks.astype(ml_dtypes.bfloat16),
        ident_in=make_consts(),
        ones_in=np.ones((128, 1), ml_dtypes.bfloat16),
    )


def kernel(**inputs):
    from concourse.bass_utils import run_bass_kernel_spmd
    nc, _ = build()
    in_maps = [stage_core(inputs, c // 2, c % 2) for c in range(8)]
    res = run_bass_kernel_spmd(nc, in_maps, core_ids=list(range(8)))
    outp = np.empty((4, 4096, 2048), np.float32)
    for c in range(8):
        outp[c // 2, (c % 2) * 2048:(c % 2) * 2048 + 2048] = res.results[c]["out"]
    return outp


# version 03
# speedup vs baseline: 1.0065x; 1.0065x over previous
import numpy as np
import ml_dtypes
import concourse.bass as bass
import concourse.mybir as mybir

F32 = mybir.dt.float32
BF16 = mybir.dt.bfloat16
U8 = mybir.dt.uint8
AF = mybir.ActivationFunctionType
ALU = mybir.AluOpType
AX = mybir.AxisListType
f32 = np.float32

D = 2048
L = 2048
TL = 2560
NT = 20
NKC = 16
FE = 1024
E = 8

ENGINES = ("sync", "gpsimd", "scalar", "vector", "tensor")


class B:
    def __init__(self, nc):
        self.nc = nc
        self.recs = {e: [] for e in ENGINES}
        self.sems = {}
        self.counts = {}
        self.dma_sems = set()
        self.maxwait = {e: {} for e in ENGINES}
        self.last_ev = {}

    def sem(self, name, dma=False):
        if name not in self.sems:
            self.sems[name] = self.nc.alloc_semaphore(name)
            self.counts[name] = 0
            if dma:
                self.dma_sems.add(name)
        return name

    def op(self, eng, fn, waits=(), inc=None, n=1):
        w = []
        for ev in waits:
            if ev is None:
                continue
            (s, v, peng) = ev
            if v <= 0:
                continue
            if self.maxwait[eng].get(s, -1) >= v:
                continue
            self.maxwait[eng][s] = v
            w.append((s, v))
        evout = None
        if inc is not None:
            self.sem(inc)
            self.counts[inc] += n
            evout = (inc, self.counts[inc], eng)
            self.last_ev[eng] = evout
        self.recs[eng].append((fn, w, inc, n))
        return evout

    def dma(self, eng, out_ap, in_ap, waits=(), slow=False, accum=False):
        qsem = "q_" + eng
        self.sem(qsem, dma=True)
        def fn(e, out_ap=out_ap, in_ap=in_ap, slow=slow, accum=accum):
            kw = {}
            if slow:
                kw["allow_slow_non_contiguous"] = True
            if accum:
                kw["accum_op"] = ALU.add
            return e.dma_start(out=out_ap, in_=in_ap, **kw)
        return self.op(eng, fn, waits=waits, inc=qsem, n=16)

    def fence(self):
        return [self.last_ev.get(e) for e in ENGINES]

    def wait_only(self, eng, waits):
        self.recs[eng].append((None, [(s, v) for (s, v, _) in waits if v > 0], None, 0))

    def emit(self):
        nc = self.nc
        with nc.Block() as block:
            for eng_name in ENGINES:
                recs = self.recs[eng_name]
                def make(recs=recs):
                    def run(eng):
                        for (fn, w, inc, n) in recs:
                            for (s, v) in w:
                                eng.wait_ge(self.sems[s], v)
                            if fn is None:
                                continue
                            inst = fn(eng)
                            if inc is not None:
                                if isinstance(inst, list):
                                    inst = inst[-1]
                                inst.then_inc(self.sems[inc], n)
                    return run
                getattr(block, eng_name)(make())


def build(taps=()):
    nc = bass.Bass("TRN2", target_bir_lowering=False)
    b = B(nc)

    def din(name, shape, dt=F32):
        return nc.dram_tensor(name, shape, dt, kind="ExternalInput").ap()
    x_in = din("x_in", [TL, D])
    wq_in = din("wq", [D, D]); wk_in = din("wk", [512, D])
    wv_in = din("wv", [512, D]); wo_in = din("wo", [D, D])
    router_in = din("router", [8, D]); normw_in = din("normw", [1, D])
    gate_in = din("gate", [E, D, FE]); up_in = din("up", [E, D, FE])
    down_in = din("down", [E, FE, D])
    c_in = din("c_row", [D]); wmod_in = din("wmod", [D, 6 * D])
    cos_in = din("cos_in", [TL, 128]); sin_in = din("sin_in", [TL, 128])
    masks_in = din("masks_in", [8, 6, 128, 256], BF16)
    ident_in = din("ident_in", [128, 128], BF16)
    ones_in = din("ones_in", [128, 1], BF16)
    out = nc.dram_tensor("out", [L, D], F32, kind="ExternalOutput").ap()

    mod_hbm = nc.dram_tensor("mod_hbm", [6, D], F32, kind="Internal").ap()
    wqT_hbm = nc.dram_tensor("wqT_hbm", [NKC, 128, D], BF16, kind="Internal").ap()
    woT_hbm = nc.dram_tensor("woT_hbm", [NKC, 128, D], BF16, kind="Internal").ap()
    qt_hbm = nc.dram_tensor("qt_hbm", [8, NKC, 128, 256], BF16, kind="Internal").ap()
    x1_hbm = nc.dram_tensor("x1_hbm", [L, D], F32, kind="Internal").ap()
    y_hbm = nc.dram_tensor("y_hbm", [L, D], F32, kind="Internal").ap()

    tap_aps = {}
    def tap(name, shape, dtype=F32):
        if name in taps:
            tap_aps[name] = nc.dram_tensor("tap_" + name, shape, dtype, kind="ExternalOutput").ap()
            return tap_aps[name]
        return None

    ARENA = 212800
    arena = nc.alloc_sbuf_tensor("arena", [128, ARENA], U8)
    av = arena[:]
    cursor = [0]
    hwm = [0]
    def buf(nbytes, dt, shape_last):
        off = (cursor[0] + 63) // 64 * 64
        cursor[0] = off + nbytes
        hwm[0] = max(hwm[0], cursor[0])
        assert cursor[0] <= ARENA, f"arena overflow {cursor[0]}"
        ap = av[:, off:off + nbytes].bitcast(dt)
        tot = 1
        for s_ in shape_last:
            tot *= s_
        assert tot == ap.shape[1], (shape_last, ap.shape)
        if len(shape_last) == 1:
            return ap
        pat = "p (" + " ".join(f"d{i}" for i in range(len(shape_last))) + ") -> p " + " ".join(f"d{i}" for i in range(len(shape_last)))
        return ap.rearrange(pat, **{f"d{i}": s_ for i, s_ in enumerate(shape_last)})

    ident = buf(256, BF16, [128])
    ones_col = buf(64, BF16, [32])[:, 0:1]
    modbc = buf(16384, F32, [2, 2048])
    stats = buf(320, F32, [80])
    cwbuf = buf(128, F32, [4, 8])
    _ls = buf(64, F32, [16])
    logit_sb = _ls[:, 0:8]
    srt8 = _ls[:, 8:16]
    PERS_END = cursor[0]

    kT_sb = buf(20480, BF16, [4, 2560])
    v_sb = buf(20480, BF16, [20, 512])
    SPAN_END = cursor[0]
    PB = (SPAN_END + 63) // 64 * 64

    pool_t = nc.alloc_psum_tensor("pool", [128, 4096], F32)
    pool = pool_t[:]
    pool_bf = pool_t[:].bitcast(BF16)
    def bank(i):
        return pool[:, i * 512:(i + 1) * 512]
    def bank_bf(i):
        return pool_bf[:, i * 1024:(i + 1) * 1024]

    QS, QG = "q_sync", "q_gp"
    b.sem(QS, dma=True); b.sem(QG, dma=True)
    SV, SS, ST = "sv", "ss", "st"

    ev_ident = b.dma("sync", ident, ident_in)
    ev_ones = b.dma("sync", ones_col, ones_in)

    cursor[0] = PB
    wm_sb = buf(32768, BF16, [2, NKC, 512])
    c_sb = buf(64, F32, [16])
    c_si = buf(32, BF16, [16])
    mrow_region = buf(49152, F32, [12288])
    mrow = mrow_region[0:1, :]
    clampscr = buf(16384, F32, [4096])
    hi_row = clampscr[0:1, 0:2048]
    lo_row = clampscr[0:1, 2048:4096]
    P0_END = cursor[0]

    ev_c = b.dma("sync", c_sb, c_in.rearrange("(k p) -> p k", p=128), slow=True)
    ev_silu = b.op("scalar", lambda e: e.activation(out=c_si, in_=c_sb, func=AF.Silu),
                   waits=[ev_c], inc=SS)
    free_wm = [None, None]
    free_p0bank = [None, None]
    ev_evac = {}
    for j in range(24):
        sl = j % 2
        ev_l = b.dma("gpsimd", wm_sb[:, sl],
                     wmod_in[:, 512 * j:512 * j + 512].rearrange("(k p) c -> p k c", p=128),
                     waits=[free_wm[sl]])
        def mmj(e, sl=sl):
            insts = []
            for k in range(NKC):
                insts.append(e.matmul(bank(sl)[0:1, :], c_si[:, k:k + 1], wm_sb[:, sl, k],
                                      start=(k == 0), stop=(k == NKC - 1)))
            return insts
        ev_mm = b.op("tensor", mmj, waits=[ev_l, ev_silu, free_p0bank[sl]], inc=ST)
        free_wm[sl] = ev_mm
        ev_evac[j] = b.op("vector",
                          lambda e, sl=sl, j=j: e.tensor_copy(out=mrow[:, 512 * j:512 * j + 512],
                                                              in_=bank(sl)[0:1, :]),
                          waits=[ev_mm], inc=SV)
        free_p0bank[sl] = ev_evac[j]
    mr = mrow.rearrange("o (six d) -> o six d", six=6)
    ev_ca1 = b.op("vector", lambda e: e.tensor_scalar_add(out=mr[:, 1], in0=mr[:, 1], scalar1=1.0),
                  waits=[ev_evac[7]], inc=SV)
    ev_cb1 = b.op("vector", lambda e: e.tensor_scalar_add(out=mr[:, 4], in0=mr[:, 4], scalar1=1.0),
                  waits=[ev_evac[19]], inc=SV)
    def clamp(idx, wait):
        c1 = b.op("vector", lambda e, idx=idx: e.tensor_scalar_max(out=hi_row, in0=mr[:, idx], scalar1=0.25), waits=[wait], inc=SV)
        c2 = b.op("vector", lambda e, idx=idx: e.tensor_scalar_min(out=lo_row, in0=mr[:, idx], scalar1=-0.25), waits=[wait, c1], inc=SV)
        c3 = b.op("vector", lambda e: e.tensor_sub(out=hi_row, in0=hi_row, in1=lo_row), waits=[c1, c2], inc=SV)
        c4 = b.op("vector", lambda e, idx=idx: e.tensor_single_scalar(out=mr[:, idx], in_=mr[:, idx], scalar=0.0, op=ALU.is_ge), waits=[c3], inc=SV)
        c5 = b.op("vector", lambda e, idx=idx: e.tensor_mul(out=hi_row, in0=hi_row, in1=mr[:, idx]), waits=[c4], inc=SV)
        return b.op("vector", lambda e, idx=idx: e.tensor_add(out=mr[:, idx], in0=hi_row, in1=lo_row), waits=[c5], inc=SV)
    ev_ga = clamp(2, ev_evac[11])
    ev_gb = clamp(5, ev_evac[23])
    ev_modspill = b.dma("sync", mod_hbm.flatten()[None, :], mrow,
                        waits=[ev_ca1, ev_cb1, ev_ga, ev_gb])
    t_mod = tap("mod", [6, D])
    if t_mod is not None:
        b.dma("sync", t_mod.flatten()[None, :], mrow,
              waits=[ev_ca1, ev_cb1, ev_ga, ev_gb])
    def bc_mod(slot, idx, waits=()):
        return b.dma("sync", modbc[:, slot], mod_hbm[idx:idx + 1, :].broadcast_to([128, D]),
                     waits=[ev_modspill] + list(waits))
    ev_bc_ca1 = bc_mod(0, 1)
    ev_bc_sa = bc_mod(1, 0)
    t_mbc = tap("mbc", [128, 3, 2048])
    if t_mbc is not None:
        b.dma("sync", t_mbc.rearrange("p s d -> p (s d)").rearrange("p (s d) -> p s d", s=3),
              modbc[:], waits=[ev_bc_ca1, ev_bc_sa])

    cursor[0] = PB
    wkT_sb = buf(16384, BF16, [NKC, 512])
    wvT_sb = buf(16384, BF16, [NKC, 512])
    xg = buf(16384, F32, [2, 2048])
    scr_a = buf(8192, F32, [2048])
    h_bf = buf(4096, BF16, [1, 2048])
    hT_sb = buf(16384, BF16, [NKC, 512])
    cossin = buf(4096, F32, [4, 256])
    q_r = buf(2048, F32, [512])
    qln_scr = buf(1024, BF16, [512])
    sq_qk = buf(2048, F32, [512])
    kln_bf = qln_scr
    qt_stage = buf(16384, BF16, [NKC, 512])
    rope_scr = buf(2048, F32, [512])
    wpc = buf(32768, BF16, [2, NKC, 512])
    A1_END = cursor[0]
    WN_OFF = (cursor[0] + 63) // 64 * 64
    wnat = buf(8192, BF16, [2, 2048])
    wt_stage = buf(4096, BF16, [NKC, 128])
    A0_END = cursor[0]
    q_r2 = av[:, WN_OFF:WN_OFF + 2048].bitcast(F32)
    sq_qk2 = av[:, WN_OFF + 2048:WN_OFF + 4096].bitcast(F32)
    rope_scr2 = av[:, WN_OFF + 4096:WN_OFF + 6144].bitcast(F32)
    qln_scr2 = av[:, WN_OFF + 6144:WN_OFF + 7168].bitcast(BF16)
    assert A0_END <= ARENA, A0_END

    free_wnat = [None, None]
    free_stg = [None, None]
    free_wt = None
    seq = 0
    jobs = ([("ksb", wk_in, oc, wkT_sb) for oc in range(4)] +
            [("vsb", wv_in, oc, wvT_sb) for oc in range(4)] +
            [("qh", wq_in, oc, wqT_hbm) for oc in range(NKC)] +
            [("oh", wo_in, oc, woT_hbm) for oc in range(NKC)])
    ev_wkv_done = None
    ev_wq_spill = None
    ev_wo_spill = {}
    for (kind, src, oc, dst) in jobs:
        sl = seq % 2
        ev_l = b.dma("gpsimd", wnat[:, sl], src[128 * oc:128 * oc + 128, :],
                     waits=[free_wnat[sl]])
        last_evac = None
        for half in range(2):
            stg = half
            mms = []
            for fi in range(8):
                fc = 8 * half + fi
                psl = bank_bf(6 + stg)[:, fi * 128:fi * 128 + 128]
                mms.append(b.op("tensor",
                                lambda e, psl=psl, sl=sl, fc=fc: e.transpose(psl, wnat[:, sl, 128 * fc:128 * fc + 128], ident),
                                waits=[ev_l, ev_ident, free_stg[stg]], inc=ST))
            src_view = bank_bf(6 + stg)[:, 0:1024].rearrange("p (f c) -> p f c", f=8)
            if kind in ("ksb", "vsb"):
                last_evac = b.op("vector",
                                 lambda e, src_view=src_view, dst=dst, half=half, oc=oc: e.tensor_copy(
                                     out=dst[:, 8 * half:8 * half + 8, 128 * oc:128 * oc + 128], in_=src_view),
                                 waits=[mms[-1]], inc=SV)
            else:
                last_evac = b.op("vector",
                                 lambda e, src_view=src_view, half=half: e.tensor_copy(
                                     out=wt_stage[:, 8 * half:8 * half + 8, :], in_=src_view),
                                 waits=[mms[-1], free_wt], inc=SV)
            free_stg[stg] = last_evac
        free_wnat[sl] = mms[-1]
        if kind in ("qh", "oh"):
            evs = b.dma("sync", dst[:, :, 128 * oc:128 * oc + 128].rearrange("fc p c -> p fc c"),
                        wt_stage[:], waits=[last_evac])
            free_wt = evs
            if kind == "qh" and oc == NKC - 1:
                ev_wq_spill = evs
            if kind == "oh":
                ev_wo_spill[oc] = evs
        if kind == "vsb" and oc == 3:
            ev_wkv_done = last_evac
        seq += 1
    ev_wo_spill_all = ev_wo_spill[NKC - 1]

    t_h = tap("h", [TL, D], BF16)
    t_qln = tap("qln", [8, NKC, 128, 256], BF16)
    t_kt = tap("kt", [4, 128, TL], BF16)
    t_v = tap("v", [TL, 512], BF16)

    free_xg = [ev_modspill, ev_modspill]
    free_scr_a = ev_modspill
    free_hbf = [None, None]
    free_hT = None
    free_cs = ev_modspill
    free_qtcol = [None, None, None, None]
    free_wpc = [None, None]
    free_bank = {2: None, 3: None, 4: None, 5: None}
    ev_hT_tile = {}
    ev_qspill = {}
    ev_ktile = {}
    ev_vtile = {}

    C_LN1 = (0, 8)
    C_LN1B = (8, 16)
    C_KV = (16, 32)
    C_Q = (32, 46)
    C_KV2 = (64, 78)
    C_Q2 = (48, 62)
    C_F = (48, 64)
    QRs = [q_r, q_r2]; SQs = [sq_qk, sq_qk2]; RSs = [rope_scr, rope_scr2]
    QLs = [qln_scr, qln_scr2]
    CKVs = [C_KV, C_KV2]; CQs = [C_Q, C_Q2]
    ev_a0_all = ev_wo_spill[NKC - 1]
    free_qr_s = [None, ev_a0_all]
    free_sq_s = [None, ev_a0_all]
    free_ql_s = [None, ev_a0_all]
    qkv_i = [0]


    def ln_full(src_ap, n, eps, wait_src, cols, sq_dump, free_dump):
        c0 = cols[0]
        s_sum = stats[:, c0:c0 + 1]; s_sq = stats[:, c0 + 1:c0 + 2]
        s_mean = stats[:, c0 + 2:c0 + 3]; s_msq = stats[:, c0 + 3:c0 + 4]
        s_var = stats[:, c0 + 4:c0 + 5]; s_s = stats[:, c0 + 5:c0 + 6]
        s_r = stats[:, c0 + 6:c0 + 7]; s_nm = stats[:, c0 + 7:c0 + 8]
        e1 = b.op("vector", lambda e: e.reduce_sum(out=s_sum, in_=src_ap, axis=AX.X), waits=wait_src, inc=SV)
        def _sq(e):
            e.memzero(s_sq)
            return e.activation(out=sq_dump, in_=src_ap, func=AF.Square, accum_out=s_sq)
        e2 = b.op("scalar", _sq, waits=list(wait_src) + (free_dump if isinstance(free_dump, list) else [free_dump]), inc=SS)
        e3 = b.op("vector", lambda e: e.tensor_scalar_mul(out=s_mean, in0=s_sum, scalar1=1.0 / n), waits=[e1], inc=SV)
        e4 = b.op("vector", lambda e: e.tensor_scalar_mul(out=s_msq, in0=s_sq, scalar1=1.0 / n), waits=[e2], inc=SV)
        e5 = b.op("vector", lambda e: e.tensor_mul(out=s_var, in0=s_mean, in1=s_mean), waits=[e3], inc=SV)
        e6 = b.op("vector", lambda e: e.tensor_sub(out=s_var, in0=s_msq, in1=s_var), waits=[e4, e5], inc=SV)
        e7 = b.op("scalar", lambda e: e.activation(out=s_s, in_=s_var, func=AF.Sqrt, bias=f32(eps)), waits=[e6], inc=SS)
        e8 = b.op("vector", lambda e: e.reciprocal(out=s_r, in_=s_s), waits=[e7], inc=SV)
        e9 = b.op("vector", lambda e: e.tensor_scalar(out=s_nm, in0=s_mean, scalar1=s_r, scalar2=-1.0,
                                                      op0=ALU.mult, op1=ALU.mult), waits=[e8], inc=SV)
        return s_r, s_nm, e9, e2

    def ln_heads(src3, nh, hd, eps, wait_src, cols, scale_mul, sq_buf, free_sq, extra_sq_wait=()):
        c0 = cols[0]
        ssum = stats[:, c0:c0 + nh]; ssq = stats[:, c0 + nh:c0 + 2 * nh]
        smean = stats[:, c0 + 2 * nh:c0 + 3 * nh]; st1 = stats[:, c0 + 3 * nh:c0 + 4 * nh]
        su = stats[:, c0 + 4 * nh:c0 + 5 * nh]; ss_ = stats[:, c0 + 5 * nh:c0 + 6 * nh]
        sr = stats[:, c0 + 6 * nh:c0 + 7 * nh]
        src2 = src3.rearrange("p h c -> p (h c)")
        f1 = b.op("vector", lambda e: e.reduce_sum(out=ssum, in_=src3, axis=AX.X), waits=wait_src, inc=SV)
        f2 = b.op("scalar", lambda e: e.activation(out=sq_buf, in_=src2, func=AF.Square),
                  waits=list(wait_src) + [free_sq, f1] + list(extra_sq_wait), inc=SS)
        f3 = b.op("vector", lambda e: e.reduce_sum(out=ssq, in_=sq_buf.rearrange("p (h c) -> p h c", h=nh), axis=AX.X), waits=[f2], inc=SV)
        ft = b.op("scalar", lambda e: e.activation(out=st1, in_=ssum, func=AF.Square), waits=[f1], inc=SS)
        fm = b.op("scalar", lambda e: e.activation(out=smean, in_=ssum, func=AF.Identity, scale=f32(1.0 / hd)), waits=[f1], inc=SS)
        fu = b.op("vector", lambda e: e.scalar_tensor_tensor(out=su, in0=ssq, scalar=float(hd), in1=st1,
                                                             op0=ALU.mult, op1=ALU.subtract), waits=[f3, ft], inc=SV)
        fs = b.op("scalar", lambda e: e.activation(out=ss_, in_=su, func=AF.Sqrt, scale=f32(1.0 / (hd * hd)), bias=f32(eps)), waits=[fu], inc=SS)
        fr = b.op("vector", lambda e: e.reciprocal(out=sr, in_=ss_), waits=[fs], inc=SV)
        if scale_mul != 1.0:
            fr = b.op("vector", lambda e: e.tensor_scalar_mul(out=sr, in0=sr, scalar1=float(scale_mul)), waits=[fr], inc=SV)
        return smean, sr, [fr, fm], f3

    def rope(pv4, dst4, cosv, sinv, scr4, waits):
        e1 = b.op("vector", lambda e: e.tensor_tensor(out=dst4[:, :, 0], in0=pv4[:, :, 0], in1=cosv, op=ALU.mult), waits=waits, inc=SV)
        e2 = b.op("vector", lambda e: e.tensor_tensor(out=scr4[:, :, 0], in0=pv4[:, :, 1], in1=sinv, op=ALU.mult), waits=waits, inc=SV)
        e3 = b.op("vector", lambda e: e.tensor_sub(out=dst4[:, :, 0], in0=dst4[:, :, 0], in1=scr4[:, :, 0]), waits=[e1, e2], inc=SV)
        e4 = b.op("vector", lambda e: e.tensor_tensor(out=dst4[:, :, 1], in0=pv4[:, :, 1], in1=cosv, op=ALU.mult), waits=waits, inc=SV)
        e5 = b.op("vector", lambda e: e.tensor_tensor(out=scr4[:, :, 1], in0=pv4[:, :, 0], in1=sinv, op=ALU.mult), waits=waits, inc=SV)
        e6 = b.op("vector", lambda e: e.tensor_add(out=dst4[:, :, 1], in0=dst4[:, :, 1], in1=scr4[:, :, 1]), waits=[e4, e5], inc=SV)
        return [e3, e6]

    for g in range(5):
        ev_cs_c = b.dma("sync", cossin[:, :, 0:128],
                        cos_in[512 * g:512 * g + 512, :].rearrange("(tt p) c -> p tt c", p=128),
                        waits=[free_cs])
        ev_cs_s = b.dma("sync", cossin[:, :, 128:256],
                        sin_in[512 * g:512 * g + 512, :].rearrange("(tt p) c -> p tt c", p=128),
                        waits=[free_cs])
        ev_cs_both = [ev_cs_c, ev_cs_s]
        rope_evs = []
        group_consumers = []

        for ts in range(4):
            tt = 4 * g + ts
            xsl = tt % 2
            hsl = 0
            ev_x = b.dma("sync", xg[:, xsl], x_in[128 * tt:128 * tt + 128, :],
                         waits=[free_xg[xsl]])
            cols = C_LN1 if tt % 2 == 0 else C_LN1B
            s_r, s_nm, ev_st9, ev_sq = ln_full(xg[:, xsl], D, 1e-5, [ev_x], cols, scr_a, free_scr_a)
            ev_xn = b.op("scalar", lambda e, xsl=xsl, s_r=s_r, s_nm=s_nm: e.activation(
                out=scr_a, in_=xg[:, xsl], func=AF.Identity, scale=s_r, bias=s_nm),
                waits=[ev_st9], inc=SS)
            free_xg[xsl] = ev_xn
            ev_t1 = b.op("vector", lambda e: e.tensor_mul(out=scr_a, in0=scr_a, in1=modbc[:, 0]),
                 waits=[ev_xn, ev_bc_ca1], inc=SV)
            ev_h = b.op("vector", lambda e, hsl=hsl: e.tensor_add(out=h_bf[:, hsl], in0=scr_a, in1=modbc[:, 1]),
                        waits=[ev_t1, ev_bc_sa, free_hbf[hsl]], inc=SV)
            free_scr_a = ev_h
            if t_h is not None:
                b.dma("sync", t_h[128 * tt:128 * tt + 128, :], h_bf[:, hsl], waits=[ev_h])
            last_mm = None
            for half in range(2):
                stg = half
                mms = []
                for fi in range(8):
                    fc = 8 * half + fi
                    psl = bank_bf(6 + stg)[:, fi * 128:fi * 128 + 128]
                    mms.append(b.op("tensor",
                                    lambda e, psl=psl, hsl=hsl, fc=fc: e.transpose(psl, h_bf[:, hsl, 128 * fc:128 * fc + 128], ident),
                                    waits=[ev_h, ev_ident, free_stg[stg]], inc=ST))
                ev_ev = b.op("vector",
                             lambda e, stg=stg, half=half, ts=ts: e.tensor_copy(
                                 out=hT_sb[:, 8 * half:8 * half + 8, 128 * ts:128 * ts + 128],
                                 in_=bank_bf(6 + stg)[:, 0:1024].rearrange("p (f c) -> p f c", f=8)),
                             waits=[mms[-1], free_hT], inc=SV)
                free_stg[stg] = ev_ev
                last_mm = mms[-1]
            ev_hT_tile[tt] = ev_ev
            free_hbf[hsl] = last_mm

        ghT = [ev_hT_tile[4 * g + t_] for t_ in range(4)]

        for ts in range(4):
            tt = 4 * g + ts
            for which in ("k", "v"):
                bk = 2 + (ts % 2) * 2 + (0 if which == "k" else 1)
                wT = wkT_sb if which == "k" else wvT_sb
                def mmkv(e, ts=ts, wT=wT, bk=bk):
                    insts = []
                    for fc in range(NKC):
                        insts.append(e.matmul(bank(bk), hT_sb[:, fc, 128 * ts:128 * ts + 128],
                                              wT[:, fc, :], start=(fc == 0), stop=(fc == NKC - 1)))
                    return insts
                ev_mm = b.op("tensor", mmkv, waits=[ghT[ts], ev_wkv_done, free_bank[bk]], inc=ST)
                group_consumers.append(ev_mm)
                if which == "k":
                    sl_ = qkv_i[0] % 2; qkv_i[0] += 1
                    qr_ = QRs[sl_]; rs_ = RSs[sl_]; sq_ = SQs[sl_]; ql_ = QLs[sl_]
                    pv4 = bank(bk).rearrange("p (hh t c) -> p hh t c", hh=2, t=2)
                    cosv = cossin[:, ts, 0:128].unsqueeze(1).broadcast_to([128, 2, 128])
                    sinv = cossin[:, ts, 128:256].unsqueeze(1).broadcast_to([128, 2, 128])
                    kr4 = qr_.rearrange("p (hh t c) -> p hh t c", hh=2, t=2)
                    scr4 = rs_.rearrange("p (hh t c) -> p hh t c", hh=2, t=2)
                    ev_rp = rope(pv4, kr4, cosv, sinv, scr4, [ev_mm, free_qr_s[sl_]] + ev_cs_both)
                    rope_evs.append(ev_rp[-1])
                    free_bank[bk] = ev_rp[-1]
                    src3 = qr_.rearrange("p (hh c) -> p hh c", hh=2)
                    smean, sr, ev_ln, ev_sqf = ln_heads(src3, 2, 256, 1e-6, ev_rp, CKVs[sl_], 1.0, sq_, free_sq_s[sl_])
                    free_sq_s[sl_] = ev_sqf
                    apls = []
                    for hh in range(2):
                        apls.append(b.op("vector", lambda e, hh=hh, src3=src3, smean=smean, sr=sr, ql_=ql_: e.tensor_scalar(
                            out=ql_[:, 256 * hh:256 * hh + 256], in0=src3[:, hh],
                            scalar1=smean[:, hh:hh + 1], scalar2=sr[:, hh:hh + 1],
                            op0=ALU.subtract, op1=ALU.mult),
                            waits=list(ev_ln) + [free_ql_s[sl_]], inc=SV))
                    free_qr_s[sl_] = apls[-1]
                    mms = []
                    for dc in range(4):
                        psl = bank_bf(6)[:, dc * 128:dc * 128 + 128]
                        mms.append(b.op("tensor", lambda e, psl=psl, dc=dc, ql_=ql_: e.transpose(
                            psl, ql_[:, 128 * dc:128 * dc + 128], ident),
                            waits=[apls[-1], free_stg[0]], inc=ST))
                    ev_ke = b.op("vector", lambda e, tt=tt: e.tensor_copy(
                        out=kT_sb[:, :, 128 * tt:128 * tt + 128],
                        in_=bank_bf(6)[:, 0:512].rearrange("p (dc c) -> p dc c", dc=4)),
                        waits=[mms[-1]], inc=SV)
                    free_stg[0] = ev_ke
                    free_ql_s[sl_] = mms[-1]
                    ev_ktile[tt] = ev_ke
                else:
                    sl_ = qkv_i[0] % 2; qkv_i[0] += 1
                    sq_ = SQs[sl_]
                    src3 = bank(bk).rearrange("p (hh c) -> p hh c", hh=2)
                    smean, sr, ev_ln, ev_sqf = ln_heads(src3, 2, 256, 1e-6, [ev_mm], CKVs[sl_], 1.0, sq_, free_sq_s[sl_])
                    free_sq_s[sl_] = ev_sqf
                    apls = []
                    for hh in range(2):
                        apls.append(b.op("vector", lambda e, hh=hh, src3=src3, tt=tt, smean=smean, sr=sr: e.tensor_scalar(
                            out=v_sb[:, tt, 256 * hh:256 * hh + 256], in0=src3[:, hh],
                            scalar1=smean[:, hh:hh + 1], scalar2=sr[:, hh:hh + 1],
                            op0=ALU.subtract, op1=ALU.mult),
                            waits=list(ev_ln), inc=SV))
                    free_bank[bk] = apls[-1]
                    ev_vtile[tt] = apls[-1]
                    if t_v is not None:
                        b.dma("sync", t_v[128 * tt:128 * tt + 128, :], v_sb[:, tt, :], waits=[apls[-1]])

        if g >= 1:
            for j in range(4):
                wsl = j % 2
                ev_wp = b.dma("gpsimd", wpc[:, wsl],
                              wqT_hbm[:, :, 512 * j:512 * j + 512].rearrange("fc p c -> p fc c"),
                              waits=[ev_wq_spill, free_wpc[wsl]])
                for ts in range(4):
                    tt = 4 * g + ts
                    bk = 2 + ((4 * j + ts) % 4)
                    def mmq(e, ts=ts, wsl=wsl, bk=bk):
                        insts = []
                        for fc in range(NKC):
                            insts.append(e.matmul(bank(bk), hT_sb[:, fc, 128 * ts:128 * ts + 128],
                                                  wpc[:, wsl, fc, :], start=(fc == 0), stop=(fc == NKC - 1)))
                        return insts
                    ev_mm = b.op("tensor", mmq, waits=[ev_wp, ghT[ts], free_bank[bk]], inc=ST)
                    group_consumers.append(ev_mm)
                    free_wpc[wsl] = ev_mm
                    sl_ = qkv_i[0] % 2; qkv_i[0] += 1
                    qr_ = QRs[sl_]; rs_ = RSs[sl_]; sq_ = SQs[sl_]; ql_ = QLs[sl_]
                    pv4 = bank(bk).rearrange("p (hh t c) -> p hh t c", hh=2, t=2)
                    cosv = cossin[:, ts, 0:128].unsqueeze(1).broadcast_to([128, 2, 128])
                    sinv = cossin[:, ts, 128:256].unsqueeze(1).broadcast_to([128, 2, 128])
                    qr4 = qr_.rearrange("p (hh t c) -> p hh t c", hh=2, t=2)
                    scr4 = rs_.rearrange("p (hh t c) -> p hh t c", hh=2, t=2)
                    ev_rp = rope(pv4, qr4, cosv, sinv, scr4, [ev_mm, free_qr_s[sl_]] + ev_cs_both)
                    rope_evs.append(ev_rp[-1])
                    free_bank[bk] = ev_rp[-1]
                    src3 = qr_.rearrange("p (hh c) -> p hh c", hh=2)
                    smean, sr, ev_ln, ev_sqf = ln_heads(src3, 2, 256, 1e-6, ev_rp, CQs[sl_], 16.0, sq_, free_sq_s[sl_])
                    free_sq_s[sl_] = ev_sqf
                    apls = []
                    for hh in range(2):
                        apls.append(b.op("vector", lambda e, hh=hh, src3=src3, smean=smean, sr=sr, ql_=ql_: e.tensor_scalar(
                            out=ql_[:, 256 * hh:256 * hh + 256], in0=src3[:, hh],
                            scalar1=smean[:, hh:hh + 1], scalar2=sr[:, hh:hh + 1],
                            op0=ALU.subtract, op1=ALU.mult),
                            waits=list(ev_ln) + [free_ql_s[sl_]], inc=SV))
                    free_qr_s[sl_] = apls[-1]
                    mms = []
                    for dc in range(4):
                        psl = bank_bf(7)[:, 512 + dc * 128:512 + dc * 128 + 128]
                        mms.append(b.op("tensor", lambda e, psl=psl, dc=dc, ql_=ql_: e.transpose(
                            psl, ql_[:, 128 * dc:128 * dc + 128], ident),
                            waits=[apls[-1], free_stg[1]], inc=ST))
                    ev_qe = b.op("vector", lambda e, j=j, ts=ts: e.tensor_copy(
                        out=qt_stage[:, 4 * j:4 * j + 4, 128 * ts:128 * ts + 128],
                        in_=bank_bf(7)[:, 512:1024].rearrange("p (dc c) -> p dc c", dc=4)),
                        waits=[mms[-1], free_qtcol[ts]], inc=SV)
                    free_stg[1] = ev_qe
                    free_ql_s[sl_] = mms[-1]
                    if ts == 3 and j == 3:
                        pass
                    ev_hT_tile[("q", g, j, ts)] = ev_qe
            for half in range(2):
                qi = 2 * (g - 1) + half
                done = [ev_hT_tile[("q", g, j, 2 * half + s_)] for j in range(4) for s_ in range(2)]
                ev_qspill[qi] = b.dma("sync", qt_hbm[qi].rearrange("fc p c -> p fc c"),
                                      qt_stage[:, :, 256 * half:256 * half + 256], waits=done)
                for s_ in range(2):
                    free_qtcol[2 * half + s_] = ev_qspill[qi]
                if t_qln is not None:
                    b.dma("sync", t_qln[qi], qt_hbm[qi], waits=[ev_qspill[qi]])
        free_hT = group_consumers[-1]
        if rope_evs:
            free_cs = rope_evs[-1]

    if t_kt is not None:
        b.dma("sync", t_kt.rearrange("dc p c -> p dc c"), kT_sb[:],
              waits=[ev_ktile[NT - 1]])

    fenceA1 = b.fence()
    cursor[0] = PB
    qtb = buf(16384, BF16, [2, NKC, 256])
    mkb = buf(6144, BF16, [2, 6, 256])
    s_t = buf(6144, F32, [1536])
    ptb = buf(6144, BF16, [2, 1536])
    o_sb = buf(16384, BF16, [4, 2048])
    otb = buf(16384, BF16, [NKC, 512])
    x1t = buf(32768, F32, [4, 2048])
    rc_sb = buf(64, F32, [16])
    wpc2 = buf(32768, BF16, [2, NKC, 512])
    A2_END = cursor[0]
    assert A2_END <= ARENA

    t_x1 = tap("x1", [L, D])
    t_osb = tap("osb", [L, D], BF16)
    ev_bc_ga = bc_mod(0, 2, [fenceA1[3]])

    free_qtb = [fenceA1[4], fenceA1[4]]
    free_mkb = [fenceA1[4], fenceA1[4]]
    free_st = fenceA1[2]
    free_ptb = [None, None]
    free_Sreg = [fenceA1[4], fenceA1[4]]
    free_obank = fenceA1[4]
    free_rsbank = fenceA1[4]
    free_osb = [None, None, None, None]
    free_otb = None
    free_x1t = [None, None, None, None]
    free_wpc2 = [fenceA1[4], fenceA1[4]]
    ev_x1sp = {}

    b.wait_only("sync", [e_ for e_ in fenceA1 if e_ is not None])
    b.wait_only("gpsimd", [e_ for e_ in fenceA1 if e_ is not None])
    b.wait_only("tensor", [e_ for e_ in fenceA1 if e_ is not None])
    b.wait_only("scalar", [e_ for e_ in fenceA1 if e_ is not None])
    b.wait_only("vector", [e_ for e_ in fenceA1 if e_ is not None])

    blocks = [(qi, h) for qi in range(8) for h in range(8)]
    ev_tanh = {}
    ev_qtb = {}
    ev_mkb = {}
    for qi in range(8):
        sl = qi % 2
        ev_qtb[qi] = b.dma("sync", qtb[:, sl], qt_hbm[qi].rearrange("fc p c -> p fc c"),
                           waits=[ev_qspill[qi], free_qtb[sl]])
        ev_mkb[qi] = b.dma("sync", mkb[:, sl], masks_in[qi].rearrange("j p c -> p j c"),
                           waits=[free_mkb[sl]])
        kneed = min(2 * qi + 5, NT - 1)
        ev_kv_need = [ev_ktile[kneed], ev_vtile[kneed]]
        for h in range(8):
            bi = 8 * qi + h
            r = bi % 2
            kv = h // 4
            def smm(e, qi=qi, h=h, r=r, kv=kv, sl=sl):
                insts = []
                for j in range(6):
                    dst = pool[:, 1536 * r + 256 * j:1536 * r + 256 * j + 256]
                    insts.append(e.matmul(dst, ident, mkb[:, sl, j], start=True, stop=False))
                    for dc in range(2):
                        insts.append(e.matmul(dst, kT_sb[:, 2 * kv + dc, 256 * qi + 128 * j:256 * qi + 128 * j + 128],
                                              qtb[:, sl, 2 * h + dc, :], start=False, stop=(dc == 1)))
                return insts
            ev_smm = b.op("tensor", smm,
                          waits=[ev_qtb[qi], ev_mkb[qi], free_Sreg[r], ev_ident] + ev_kv_need, inc=ST)
            ev_th = b.op("scalar", lambda e, r=r: e.activation(
                out=s_t, in_=pool[:, 1536 * r:1536 * r + 1536], func=AF.Tanh, scale=f32(1.0 / 30.0)),
                waits=[ev_smm, free_st], inc=SS)
            free_Sreg[r] = ev_th
            ev_tanh[bi] = ev_th
            psl = bi % 2
            ev_exp = b.op("scalar", lambda e, psl=psl: e.activation(
                out=ptb[:, psl], in_=s_t, func=AF.Exp, scale=f32(30.0), bias=f32(-30.0)),
                waits=[free_ptb[psl]], inc=SS)
            free_st = ev_exp
            def rsmm(e, psl=psl):
                insts = []
                for qs in range(2):
                    dstc = pool[:, 3584 + qs:3584 + qs + 1]
                    for j in range(6):
                        insts.append(e.matmul(dstc, ptb[:, psl, 256 * j + 128 * qs:256 * j + 128 * qs + 128],
                                              ones_col, start=(j == 0), stop=(j == 5)))
                return insts
            ev_rs = b.op("tensor", rsmm, waits=[ev_exp, free_rsbank], inc=ST)
            ev_rc = b.op("vector", lambda e: e.reciprocal(out=rc_sb[:, 0:2], in_=pool[:, 3584:3586]),
                         waits=[ev_rs], inc=SV)
            free_rsbank = ev_rc
            def pvmm(e, qi=qi, h=h, kv=kv, psl=psl):
                insts = []
                for qs in range(2):
                    dst = pool[:, 3072 + 256 * qs:3072 + 256 * qs + 256]
                    for j in range(6):
                        insts.append(e.matmul(dst, ptb[:, psl, 256 * j + 128 * qs:256 * j + 128 * qs + 128],
                                              v_sb[:, 2 * qi + j, 256 * kv:256 * kv + 256],
                                              start=(j == 0), stop=(j == 5)))
                return insts
            ev_pv = b.op("tensor", pvmm, waits=[ev_exp, free_obank], inc=ST)
            free_ptb[psl] = ev_pv
            evs = []
            for qs in range(2):
                osl = 2 * (qi % 2) + qs
                evs.append(b.op("scalar", lambda e, qs=qs, h=h, osl=osl: e.activation(
                    out=o_sb[:, osl, 256 * h:256 * h + 256], in_=pool[:, 3072 + 256 * qs:3072 + 256 * qs + 256],
                    func=AF.Identity, scale=rc_sb[:, qs:qs + 1], bias=f32(0.0)),
                    waits=[ev_pv, ev_rc, free_osb[osl]], inc=SS))
            free_obank = evs[-1]
            ev_tanh[("oev", bi)] = evs[-1]

        if qi % 2 == 1:
            grp = qi // 2
            oev = [ev_tanh[("oev", 8 * (2 * grp) + hh)] for hh in range(8)] + \
                  [ev_tanh[("oev", 8 * (2 * grp + 1) + hh)] for hh in range(8)]
            if t_osb is not None:
                b.dma("sync", t_osb[512 * grp:512 * grp + 512, :].rearrange("(ts p) d -> p ts d", p=128),
                      o_sb[:], waits=oev)
            ev_ot = {}
            for tsub in range(4):
                for half in range(2):
                    stgb = half
                    mms = []
                    for fi in range(8):
                        oc = 8 * half + fi
                        psl = pool_bf[:, stgb * 1024 + fi * 128: stgb * 1024 + fi * 128 + 128]
                        w_ = oev + [free_Sreg[0], free_Sreg[1], free_otb]
                        mms.append(b.op("tensor", lambda e, psl=psl, tsub=tsub, oc=oc: e.transpose(
                            psl, o_sb[:, tsub, 128 * oc:128 * oc + 128], ident),
                            waits=w_ + ([ev_ot[(tsub, half, "e")]] if (tsub, half, "e") in ev_ot else
                                        ([ev_ot[(tsub - 1, half, "e")]] if (tsub - 1, half, "e") in ev_ot else [])),
                            inc=ST))
                    ev_ot[(tsub, half)] = mms[-1]
                    ev_ot[(tsub, half, "e")] = b.op("vector",
                        lambda e, stgb=stgb, half=half, tsub=tsub: e.tensor_copy(
                            out=otb[:, 8 * half:8 * half + 8, 128 * tsub:128 * tsub + 128],
                            in_=pool_bf[:, stgb * 1024:stgb * 1024 + 1024].rearrange("p (f c) -> p f c", f=8)),
                        waits=[mms[-1]], inc=SV)
            ev_ot_all = [ev_ot[(tsub, half, "e")] for tsub in range(4) for half in range(2)]
            free_Sreg[0] = ev_ot_all[-1]
            free_Sreg[1] = ev_ot_all[-1]
            free_osb = [ev_ot[(tsub, 1)] for tsub in range(4)]
            for j in range(4):
                wsl = j % 2
                ev_wp = b.dma("gpsimd", wpc2[:, wsl],
                              woT_hbm[:, :, 512 * j:512 * j + 512].rearrange("fc p c -> p fc c"),
                              waits=[ev_wo_spill_all, free_wpc2[wsl]])
                for tsub in range(4):
                    bk = 4 + ((4 * j + tsub) % 2)
                    def mmo(e, tsub=tsub, wsl=wsl, bk=bk):
                        insts = []
                        for oc in range(NKC):
                            insts.append(e.matmul(bank(bk), otb[:, oc, 128 * tsub:128 * tsub + 128],
                                                  wpc2[:, wsl, oc, :], start=(oc == 0), stop=(oc == NKC - 1)))
                        return insts
                    ev_mo = b.op("tensor", mmo, waits=[ev_wp, ev_ot_all[-1],
                                                       free_bank.get(("a2", bk))], inc=ST)
                    free_wpc2[wsl] = ev_mo
                    def resid(e, bk=bk, j=j, tsub=tsub):
                        return e.tensor_mul(out=x1t[:, tsub, 512 * j:512 * j + 512],
                                            in0=bank(bk), in1=modbc[:, 0][:, 512 * j:512 * j + 512])
                    ev_r1 = b.op("vector", resid, waits=[ev_mo, ev_bc_ga, free_x1t[tsub]], inc=SV)
                    free_bank[("a2", bk)] = ev_r1
                    if j == 3:
                        tloc = 512 * grp + 128 * tsub
                        evsp = b.dma("sync", x1_hbm[tloc:tloc + 128, :], x1t[:, tsub],
                                     waits=[ev_r1])
                        ev_x1sp[(grp, tsub)] = evsp
                        free_x1t[tsub] = evsp
            free_otb = free_wpc2[(3) % 2]
            free_Sreg[0] = free_bank[("a2", 4)]
            free_Sreg[1] = free_bank[("a2", 5)]
        free_qtb[sl] = ev_smm
        free_mkb[sl] = ev_smm
    fenceA2 = b.fence()
    cursor[0] = PERS_END
    h2T = buf(65536, BF16, [NKC, 2048])
    wgu = buf(32768, BF16, [2, NKC, 512])
    cw16 = buf(512, F32, [16, 8])
    REGA = cursor[0]
    xg3 = buf(16384, F32, [2, 2048])
    scr3 = buf(8192, F32, [2048])
    h_bf3 = buf(4096, BF16, [2048])
    rwb = buf(4096, BF16, [2048])
    rwT = buf(256, BF16, [NKC, 8])
    REGA_END = cursor[0]
    cursor[0] = REGA
    a_bf = buf(32768, BF16, [8, 2048])
    wd = buf(32768, BF16, [4, 8, 512])
    sil = buf(8192, F32, [4, 512])
    ystage = buf(16384, F32, [2, 2048])
    F_END = max(cursor[0], REGA_END)
    assert F_END <= ARENA, F_END
    r8f = av[:, REGA:REGA + 8192].bitcast(F32)
    nbf = av[:, REGA + 8192:REGA + 16384].bitcast(F32)

    t_cw = tap("cw", [L, 8])
    t_h2 = tap("h2", [L, D], BF16)

    b.wait_only("sync", [e_ for e_ in fenceA2 if e_ is not None])
    b.wait_only("gpsimd", [e_ for e_ in fenceA2 if e_ is not None])
    b.wait_only("tensor", [e_ for e_ in fenceA2 if e_ is not None])
    b.wait_only("scalar", [e_ for e_ in fenceA2 if e_ is not None])
    b.wait_only("vector", [e_ for e_ in fenceA2 if e_ is not None])

    ev_r8 = b.dma("sync", r8f[0:8, :], router_in)
    ev_nb = b.dma("sync", nbf[0:8, :], normw_in.broadcast_to([8, D]))
    ev_rw = b.op("vector", lambda e: e.tensor_mul(out=r8f[0:8, :], in0=r8f[0:8, :], in1=nbf[0:8, :]),
                 waits=[ev_r8, ev_nb], inc=SV)
    ev_rwb = b.op("vector", lambda e: e.tensor_copy(out=rwb[0:8, :], in_=r8f[0:8, :]),
                  waits=[ev_rw], inc=SV)
    mmsr = []
    for fc in range(NKC):
        psl = pool_bf[:, 6 * 1024 + fc * 8: 6 * 1024 + fc * 8 + 8]
        mmsr.append(b.op("tensor", lambda e, psl=psl, fc=fc: e.transpose(
            psl, rwb[0:8, 128 * fc:128 * fc + 128], ident[0:8, 0:8]),
            waits=[ev_rwb], inc=ST))
    ev_rwT = b.op("vector", lambda e: e.tensor_copy(
        out=rwT[:], in_=pool_bf[:, 6 * 1024:6 * 1024 + 128].rearrange("p (fc x) -> p fc x", fc=16)),
        waits=[mmsr[-1]], inc=SV)

    ev_bc_cb1 = bc_mod(0, 4, [fenceA2[3]])
    ev_bc_sb = bc_mod(1, 3, [fenceA2[3]])

    free_xg3 = [ev_rwb, ev_rwb]
    free_scr3 = ev_nb
    free_hbf3 = None
    free_stgF = [ev_rwT, fenceA2[3]]
    free_logitb = ev_rwT
    free_h2Tcol = None
    ev_h2T_all = None
    ev_cw_all = []
    ev_rms = {}
    SCALE_CW = 4.0

    for fg in range(4):
        ev_h2T_t = {}
        free_logit = None
        free_d12 = None
        ev_cw = {}
        for ts in range(4):
            gt = 4 * fg + ts
            tl = 128 * gt
            xsl = ts % 2
            ev_x1 = b.dma("sync", xg3[:, xsl], x1_hbm[tl:tl + 128, :],
                          waits=[ev_x1sp[(fg, ts)], free_xg3[xsl]])
            ev_xr = b.dma("sync", scr3, x_in[512 + tl:512 + tl + 128, :],
                          waits=[free_scr3])
            ev_add = b.op("vector", lambda e, xsl=xsl: e.tensor_add(out=xg3[:, xsl], in0=xg3[:, xsl], in1=scr3),
                          waits=[ev_x1, ev_xr], inc=SV)
            if t_x1 is not None:
                b.dma("sync", t_x1[tl:tl + 128, :], xg3[:, xsl], waits=[ev_add])
            cols = C_LN1 if ts % 2 == 0 else C_LN1B
            s_r, s_nm, ev_st9, _ = ln_full(xg3[:, xsl], D, 1e-5, [ev_add], cols, scr3, [ev_add])
            ev_xn = b.op("scalar", lambda e, xsl=xsl, s_r=s_r, s_nm=s_nm: e.activation(
                out=scr3, in_=xg3[:, xsl], func=AF.Identity, scale=s_r, bias=s_nm),
                waits=[ev_st9], inc=SS)
            ev_m1 = b.op("vector", lambda e: e.tensor_mul(out=scr3, in0=scr3, in1=modbc[:, 0]),
                 waits=[ev_xn, ev_bc_cb1], inc=SV)
            ev_h2 = b.op("vector", lambda e: e.tensor_add(out=scr3, in0=scr3, in1=modbc[:, 1]),
                         waits=[ev_m1, ev_bc_sb], inc=SV)
            c0 = C_F[0]
            s_sq = stats[:, c0 + 8 + ts:c0 + 9 + ts]
            def _sqr(e, s_sq=s_sq, xsl=xsl):
                e.memzero(s_sq)
                return e.activation(out=xg3[:, xsl], in_=scr3, func=AF.Square, accum_out=s_sq)
            ev_sq = b.op("scalar", _sqr, waits=[ev_h2], inc=SS)
            free_xg3[xsl] = ev_sq
            s_rms = stats[:, c0 + ts:c0 + ts + 1]
            ev_rm1 = b.op("vector", lambda e, s_sq=s_sq, s_rms=s_rms: e.tensor_scalar_mul(out=s_rms, in0=s_sq, scalar1=1.0 / D), waits=[ev_sq], inc=SV)
            ev_sq2 = b.op("scalar", lambda e, s_rms=s_rms: e.activation(out=s_rms, in_=s_rms, func=AF.Sqrt, bias=f32(1e-6)),
                          waits=[ev_rm1], inc=SS)
            ev_rms[ts] = b.op("vector", lambda e, s_rms=s_rms: e.reciprocal(out=s_rms, in_=s_rms), waits=[ev_sq2], inc=SV)
            ev_hb = b.op("vector", lambda e: e.tensor_copy(out=h_bf3, in_=scr3), waits=[ev_h2, free_hbf3], inc=SV)
            free_scr3 = ev_hb
            if t_h2 is not None:
                b.dma("sync", t_h2[tl:tl + 128, :], h_bf3, waits=[ev_hb])
            last_mm = None
            for half in range(2):
                stg = half
                mms = []
                for fi in range(8):
                    fc = 8 * half + fi
                    psl = bank_bf(6 + stg)[:, fi * 128:fi * 128 + 128]
                    mms.append(b.op("tensor",
                                    lambda e, psl=psl, fc=fc: e.transpose(psl, h_bf3[:, 128 * fc:128 * fc + 128], ident),
                                    waits=[ev_hb, free_stgF[stg]], inc=ST))
                ev_ev = b.op("vector",
                             lambda e, stg=stg, half=half, gt=gt: e.tensor_copy(
                                 out=h2T[:, 8 * half:8 * half + 8, 128 * gt:128 * gt + 128],
                                 in_=bank_bf(6 + stg)[:, 0:1024].rearrange("p (f c) -> p f c", f=8)),
                             waits=[mms[-1], free_h2Tcol], inc=SV)
                free_stgF[stg] = ev_ev
                last_mm = mms[-1]
            free_hbf3 = last_mm
            ev_h2T_t[ts] = ev_ev
        ev_h2T_all = ev_h2T_t[3]
        for ts in range(4):
            gt = 4 * fg + ts
            def rmm(e, gt=gt):
                insts = []
                for fc in range(NKC):
                    insts.append(e.matmul(pool[:, 3840:3848],
                                          h2T[:, fc, 128 * gt:128 * gt + 128],
                                          rwT[:, fc, :], start=(fc == 0), stop=(fc == NKC - 1)))
                return insts
            ev_lmm = b.op("tensor", rmm, waits=[ev_h2T_t[3], ev_rwT, free_logitb], inc=ST)
            c0 = C_F[0]
            s_rms = stats[:, c0 + ts:c0 + ts + 1]
            ev_lg = b.op("scalar", lambda e, s_rms=s_rms: e.activation(
                out=logit_sb, in_=pool[:, 3840:3848], func=AF.Identity, scale=s_rms, bias=f32(0.0)),
                waits=[ev_lmm, ev_rms[ts], free_logit], inc=SS)
            free_logitb = ev_lg
            ev_m8 = b.op("vector", lambda e: e.max(out=srt8, in_=logit_sb), waits=[ev_lg], inc=SV)
            d12 = stats[:, c0 + 12:c0 + 13]
            w1 = stats[:, c0 + 13:c0 + 14]
            w2 = stats[:, c0 + 14:c0 + 15]
            r1 = b.op("vector", lambda e, d12=d12: e.tensor_sub(out=d12, in0=srt8[:, 0:1], in1=srt8[:, 1:2]), waits=[ev_m8, free_d12], inc=SV)
            ev_tw = b.op("scalar", lambda e, d12=d12, w1=w1: e.activation(out=w1, in_=d12, func=AF.Tanh, scale=f32(0.5)),
                         waits=[r1], inc=SS)
            r2 = b.op("vector", lambda e, w1=w1: e.tensor_scalar(out=w1, in0=w1, scalar1=0.5 * SCALE_CW, scalar2=0.5 * SCALE_CW, op0=ALU.mult, op1=ALU.add), waits=[ev_tw], inc=SV)
            r3 = b.op("vector", lambda e, w1=w1, w2=w2: e.tensor_scalar(out=w2, in0=w1, scalar1=-1.0, scalar2=SCALE_CW, op0=ALU.mult, op1=ALU.add), waits=[r2], inc=SV)
            m1 = stats[:, 72:80]
            r4 = b.op("vector", lambda e, m1=m1: e.tensor_single_scalar(out=m1, in_=logit_sb, scalar=srt8[:, 0:1], op=ALU.is_ge), waits=[ev_m8], inc=SV)
            m2 = cw16[:, gt]
            r5 = b.op("vector", lambda e, m2=m2: e.tensor_single_scalar(out=m2, in_=logit_sb, scalar=srt8[:, 1:2], op=ALU.is_ge), waits=[ev_m8], inc=SV)
            r6 = b.op("vector", lambda e, m1=m1, m2=m2: e.tensor_sub(out=m2, in0=m2, in1=m1), waits=[r4, r5], inc=SV)
            r7 = b.op("vector", lambda e, m1=m1, w1=w1: e.tensor_scalar_mul(out=m1, in0=m1, scalar1=w1), waits=[r2, r6], inc=SV)
            ev_cw_all.append(b.op("vector", lambda e, m1=m1, m2=m2, w2=w2: e.scalar_tensor_tensor(
                out=m2, in0=m2, scalar=w2, in1=m1, op0=ALU.mult, op1=ALU.add), waits=[r3, r7], inc=SV))
            free_logit = r5
            free_d12 = ev_tw
        free_stgF[1] = free_logitb
        if t_cw is not None:
            b.dma("sync", t_cw[512 * fg:512 * fg + 512, :].rearrange("(ts p) x -> p ts x", p=128),
                  cw16[:, 4 * fg:4 * fg + 4], waits=[ev_cw_all[-1]])
        free_h2Tcol = free_h2Tcol

    ev_cw_last = ev_cw_all[-1]
    fenceSPA = b.fence()

    free_wgu = [fenceSPA[4], fenceSPA[4]]
    free_wd = [fenceSPA[1] for _ in range(4)]
    free_gbank = {i: fenceSPA[3] for i in range(8)}
    free_sil = fenceSPA[2]
    free_abf = {f_: fenceSPA[4] for f_ in range(8)}
    free_ystage = [fenceSPA[1], fenceSPA[1]]
    y_tile_chain = {t_: None for t_ in range(16)}

    for e_i in range(E):
        ev_load = {}
        ev_load[("g", 0)] = b.dma("gpsimd", wgu[:, 0],
                                  gate_in[e_i, :, 0:512].rearrange("(dc p) f -> p dc f", p=128),
                                  waits=[free_wgu[0], fenceSPA[3], fenceSPA[2]])
        ev_load[("u", 0)] = b.dma("gpsimd", wgu[:, 1],
                                  up_in[e_i, :, 0:512].rearrange("(dc p) f -> p dc f", p=128),
                                  waits=[free_wgu[1]])
        last_gmm = None
        last_umm = None
        last_att = None
        for hf_ in range(2):
            if hf_ == 1:
                ev_load[("g", 1)] = b.dma("gpsimd", wgu[:, 0],
                                          gate_in[e_i, :, 512:1024].rearrange("(dc p) f -> p dc f", p=128),
                                          waits=[last_gmm])
                ev_load[("u", 1)] = b.dma("gpsimd", wgu[:, 1],
                                          up_in[e_i, :, 512:1024].rearrange("(dc p) f -> p dc f", p=128),
                                          waits=[last_umm])
            for fq in range(4):
                fcp = 4 * hf_ + fq
                def gmm(e, fq=fq):
                    insts = []
                    for dc in range(NKC):
                        for tp in range(4):
                            insts.append(e.matmul(bank(tp), wgu[:, 0, dc, 128 * fq:128 * fq + 128],
                                                  h2T[:, dc, 512 * tp:512 * tp + 512],
                                                  start=(dc == 0), stop=(dc == NKC - 1)))
                    return insts
                ev_gm = b.op("tensor", gmm, waits=[ev_load[("g", hf_)], ev_h2T_all] +
                             [free_gbank[tp] for tp in range(4)], inc=ST)
                last_gmm = ev_gm
                ev_sls = []
                for tp in range(4):
                    ev_sls.append(b.op("scalar", lambda e, tp=tp: e.activation(out=sil[:, tp], in_=bank(tp), func=AF.Silu),
                                       waits=[ev_gm, free_sil], inc=SS))
                def umm(e, fq=fq):
                    insts = []
                    for dc in range(NKC):
                        for tp in range(4):
                            insts.append(e.matmul(bank(4 + tp), wgu[:, 1, dc, 128 * fq:128 * fq + 128],
                                                  h2T[:, dc, 512 * tp:512 * tp + 512],
                                                  start=(dc == 0), stop=(dc == NKC - 1)))
                    return insts
                ev_um = b.op("tensor", umm, waits=[ev_load[("u", hf_)], fenceSPA[2]] +
                             [free_gbank[4 + tp] for tp in range(4)], inc=ST)
                last_umm = ev_um
                ev_ats = []
                for tp in range(4):
                    ev_ats.append(b.op("vector", lambda e, tp=tp, fcp=fcp: e.tensor_mul(
                        out=a_bf[:, fcp, 512 * tp:512 * tp + 512], in0=sil[:, tp], in1=bank(4 + tp)),
                        waits=[ev_sls[tp], ev_um, free_abf[fcp]], inc=SV))
                for tp in range(4):
                    free_gbank[tp] = ev_sls[tp]
                    free_gbank[4 + tp] = ev_ats[tp]
                free_sil = ev_ats[-1]
                last_att = ev_ats[-1]
                free_abf[fcp] = None
        for dp in range(4):
            ev_load[("d", dp)] = b.dma("gpsimd", wd[:, dp],
                                       down_in[e_i, :, 512 * dp:512 * dp + 512].rearrange("(fc p) d -> p fc d", p=128),
                                       waits=[free_wd[dp]])
        last_eo = {}
        for tsub in range(16):
            par = tsub % 2
            bks = [4 * par + i for i in range(4)]
            def emm(e, tsub=tsub, bks=bks):
                insts = []
                for fcp in range(8):
                    for di, dp in enumerate(range(4)):
                        insts.append(e.matmul(bank(bks[di]), a_bf[:, fcp, 128 * tsub:128 * tsub + 128],
                                              wd[:, dp, fcp, :], start=(fcp == 0), stop=(fcp == 7)))
                return insts
            ev_em = b.op("tensor", emm, waits=[last_att] + [ev_load[("d", dp)] for dp in range(4)] +
                         [free_gbank[bk] for bk in bks], inc=ST)
            ysl = tsub % 2
            evacs = []
            for di in range(4):
                evacs.append(b.op("scalar", lambda e, di=di, bks=bks, ysl=ysl, tsub=tsub, e_i=e_i: e.activation(
                    out=ystage[:, ysl, 512 * di:512 * di + 512], in_=bank(bks[di]),
                    func=AF.Identity, scale=cw16[:, tsub, e_i:e_i + 1], bias=f32(0.0)),
                    waits=[ev_em, ev_cw_last, free_ystage[ysl]], inc=SS))
            for di in range(4):
                free_gbank[bks[di]] = evacs[di]
            evacc = b.dma("gpsimd", y_hbm[128 * tsub:128 * tsub + 128, :], ystage[:, ysl],
                          waits=[evacs[-1], y_tile_chain[tsub]], accum=(e_i > 0))
            y_tile_chain[tsub] = evacc
            free_ystage[ysl] = evacc
            last_eo[tsub] = ev_em
        for dp in range(4):
            free_wd[dp] = last_eo[15]
        free_wgu[0] = last_gmm
        free_wgu[1] = last_umm
        for f_ in range(8):
            free_abf[f_] = last_eo[15]

    ev_bc_gb = bc_mod(1, 5, [fenceSPA[3]])
    free_oxg = [fenceSPA[0], fenceSPA[0]]
    free_oscr = fenceSPA[0]
    for tsub in range(16):
        tl = 128 * tsub
        ysl = tsub % 2
        ev_y = b.dma("sync", ystage[:, ysl], y_hbm[tl:tl + 128, :],
                     waits=[y_tile_chain[tsub], free_ystage[ysl]])
        xsl = tsub % 2
        ev_x2 = b.dma("sync", xg3[:, xsl], x_in[512 + tl:512 + tl + 128, :],
                      waits=[free_oxg[xsl]])
        ev_sp = b.dma("sync", scr3, x1_hbm[tl:tl + 128, :], waits=[free_oscr])
        ev_o1 = b.op("vector", lambda e, ysl=ysl: e.tensor_mul(
            out=ystage[:, ysl], in0=ystage[:, ysl], in1=modbc[:, 1]),
            waits=[ev_y, ev_bc_gb], inc=SV)
        ev_o2 = b.op("vector", lambda e, ysl=ysl, xsl=xsl: e.tensor_add(
            out=ystage[:, ysl], in0=ystage[:, ysl], in1=xg3[:, xsl]),
            waits=[ev_o1, ev_x2], inc=SV)
        ev_o3 = b.op("vector", lambda e, ysl=ysl: e.tensor_add(
            out=ystage[:, ysl], in0=ystage[:, ysl], in1=scr3),
            waits=[ev_o2, ev_sp], inc=SV)
        free_oxg[xsl] = ev_o2
        free_oscr = ev_o3
        ev_st_ = b.dma("sync", out[tl:tl + 128, :], ystage[:, ysl], waits=[ev_o3])
        free_ystage[ysl] = ev_st_

    b.wait_only("gpsimd", [(QS, b.counts[QS], "sync"), (QG, b.counts[QG], "gpsimd")])
    b.emit()
    return nc, tap_aps


def make_consts():
    ident = np.eye(128, dtype=ml_dtypes.bfloat16)
    return ident


def stage_core(inputs, b_, hf):
    S = 4096
    t0 = hf * 2048
    x = np.asarray(inputs["L0_attn_hidden_states"][b_])
    xl = np.zeros((TL, D), np.float32)
    lo = t0 - 512
    if lo < 0:
        xl[512:] = x[t0:t0 + 2048]
    else:
        xl[:] = x[lo:t0 + 2048]
    pos = np.arange(t0 - 512, t0 + 2048, dtype=np.float64)
    inv = 1.0 / (1e6 ** (np.arange(0, 256, 2, dtype=np.float64) / 256.0))
    fr = pos[:, None] * inv[None, :]
    cos_t = np.cos(fr).astype(np.float32)
    sin_t = np.sin(fr).astype(np.float32)
    masks = np.zeros((8, 6, 128, 256), np.float32)
    for qi in range(8):
        for j in range(6):
            r = np.arange(128)[:, None]
            c = np.arange(256)[None, :]
            key_g = t0 - 512 + 256 * qi + 128 * j + r
            qry_g = t0 + 256 * qi + c
            allowed = (key_g <= qry_g) & (key_g > qry_g - 512) & (key_g >= 0)
            masks[qi, j] = np.where(allowed, 0.0, -32768.0)
    return dict(
        x_in=xl,
        wq=np.asarray(inputs["L0_attn_Wq"]),
        wk=np.asarray(inputs["L0_attn_Wk"]),
        wv=np.asarray(inputs["L0_attn_Wv"]),
        wo=np.asarray(inputs["L0_attn_Wo"]),
        router=np.asarray(inputs["L0_ffn_router_w"]),
        normw=np.asarray(inputs["L0_ffn_norm_w"]).reshape(1, D),
        gate=np.asarray(inputs["L0_ffn_gate_w"]),
        up=np.asarray(inputs["L0_ffn_up_w"]),
        down=np.asarray(inputs["L0_ffn_down_w"]),
        c_row=np.asarray(inputs["L0_c"][b_]),
        wmod=np.asarray(inputs["L0_wmod"]),
        cos_in=cos_t, sin_in=sin_t,
        masks_in=masks.astype(ml_dtypes.bfloat16),
        ident_in=make_consts(),
        ones_in=np.ones((128, 1), ml_dtypes.bfloat16),
    )


def kernel(**inputs):
    from concourse.bass_utils import run_bass_kernel_spmd
    nc, _ = build()
    in_maps = [stage_core(inputs, c // 2, c % 2) for c in range(8)]
    res = run_bass_kernel_spmd(nc, in_maps, core_ids=list(range(8)))
    outp = np.empty((4, 4096, 2048), np.float32)
    for c in range(8):
        outp[c // 2, (c % 2) * 2048:(c % 2) * 2048 + 2048] = res.results[c]["out"]
    return outp
